# version 04
# baseline (speedup 1.0000x reference)
import numpy as np
import ml_dtypes
from contextlib import ExitStack
import concourse.bass as bass
import concourse.mybir as mybir
from concourse.bass_utils import run_bass_kernel_spmd

F32 = mybir.dt.float32
BF16 = mybir.dt.bfloat16
I8 = mybir.dt.int8
AF = mybir.ActivationFunctionType
ALU = mybir.AluOpType
AX = mybir.AxisListType

E, H, D, M = 1024, 16, 64, 256
B, NSEQ, NT = 4, 4096, 2048
NCH, KCH = 16, 32
S2C = float(D ** (-0.25))
KEPS = 1e-4
LNK = float(np.log(KEPS))
LNEPS = 1e-5
BF = ml_dtypes.bfloat16

_TAPS = ()
_STOP = None


def _bfc(a):
    return np.ascontiguousarray(np.asarray(a, np.float32)).astype(BF)


class Sem:
    __slots__ = ("h", "n")

    def __init__(self, ctx, nc, name):
        self.h = ctx.enter_context(nc.semaphore(name))
        self.n = 0

    def inc(self, instr, k=1):
        instr.then_inc(self.h, k)
        self.n += k
        return instr

    def wait(self, eng, n=None):
        n = self.n if n is None else n
        assert n <= self.n, ("wait beyond emitted", n, self.n)
        if n > 0:
            eng.wait_ge(self.h, n)


OB = 16512


class DPool:

    def __init__(self, ctx, nc, name, k=4):
        self.sems = [Sem(ctx, nc, name + str(i)) for i in range(k)]
        self.i = 0
        self.cur = None

    def begin(self, eng):
        s = self.sems[self.i]
        self.i = (self.i + 1) % len(self.sems)
        if s.n:
            eng.wait_ge(s.h, s.n)
        self.cur = s
        return s

    def dma(self, eng, out, in_, **kw):
        s = self.cur
        eng.dma_start(out=out, in_=in_, **kw).then_inc(s.h, 16)
        s.n += 16

    def tok(self):
        return (self.cur, self.cur.n)

    def wait_all(self, eng):
        for s in self.sems:
            if s.n:
                eng.wait_ge(s.h, s.n)


def tw(eng, t):
    s, n = t
    assert n <= s.n
    eng.wait_ge(s.h, n)


class Arena:
    def __init__(self, nc, base, end):
        base, end = OB + base, OB + end
        self.nc, self.base, self.end, self.off = nc, base, end, base

    def at(self, name, shape, dtype):
        per = int(np.prod(shape[1:])) * mybir.dt.size(dtype)
        off = (self.off + 31) // 32 * 32
        h = self.nc.alloc_sbuf_tensor_at(name, shape, dtype, offset=off)
        self.off = off + per
        assert self.off <= self.end, (name, self.off, self.end)
        return h


def build_nc(taps=()):
    nc = bass.Bass("TRN2", target_bir_lowering=False, detect_race_conditions=False)
    dt = nc.dram_tensor

    x_own = dt("x_own", [NT, E], F32, kind="ExternalInput").ap()
    x_par = dt("x_par", [NT, E], F32, kind="ExternalInput").ap()
    wqT = dt("wqT", [E, E], F32, kind="ExternalInput").ap()
    wkT = dt("wkT", [E, 512], F32, kind="ExternalInput").ap()
    wvT = dt("wvT", [E, 512], F32, kind="ExternalInput").ap()
    woTd = dt("woT", [E, E], F32, kind="ExternalInput").ap()
    projTd = dt("projT", [128, 2 * M], F32, kind="ExternalInput").ap()
    bq_row = dt("bq_row", [1, E], F32, kind="ExternalInput").ap()
    bk_row = dt("bk_row", [1, 512], F32, kind="ExternalInput").ap()
    bv_row = dt("bv_row", [1, 512], F32, kind="ExternalInput").ap()
    bo_row = dt("bo_row", [1, E], F32, kind="ExternalInput").ap()
    w8Td = dt("w8T", [E, E], I8, kind="ExternalInput").ap()
    ws_row = dt("ws_row", [1, E], F32, kind="ExternalInput").ap()
    fb_row = dt("fb_row", [1, E], F32, kind="ExternalInput").ap()
    y_out = dt("y", [NT, E], F32, kind="ExternalOutput").ap()

    d_qp = dt("d_qp", [8, 128, NT], BF16).ap()
    d_vtok = dt("d_vtok", [KCH, 128, 512], BF16).ap()
    d_dqT = dt("d_dqT", [NCH, 128, 16], F32).ap()
    ag_in = dt("ag_in", [8, 66, M], F32)
    ag_out = dt("ag_out", [16, 66, M], F32)
    mxk_in = dt("mxk_in", [1, 8], F32)
    mxk_out = dt("mxk_out", [1, 8], F32)
    mxx_in = dt("mxx_in", [1, 8], F32)
    mxx_out = dt("mxx_out", [1, 8], F32)

    tap = {}
    tapinfo = {
        "ln1T": ([8, 128, NSEQ], BF16), "qp": ([8, 128, NT], BF16),
        "vtok": ([KCH, 128, 512], BF16), "kp": ([4, 128, NSEQ], BF16),
        "edk": ([128, KCH, 8], F32), "dqT": ([NCH, 128, 16], F32),
        "gx": ([8, 66, M], F32), "mxk": ([1, 8], F32), "gf": ([16, 66, M], F32),
        "caug": ([128, 2, 16, 66], BF16), "attn": ([128, 8, NT], BF16),
        "xmid": ([128, NCH, E], F32), "ln2": ([128, NCH, E], F32),
        "mxx": ([1, 8], F32), "eqt": ([128, 8, 2, NT], BF16),
        "epd": ([128, 1048], F32),
    }
    for t in taps:
        shp, dtp = tapinfo[t]
        tap[t] = dt("dbg_" + t, shp, dtp, kind="ExternalOutput").ap()

    ident_c = nc.inline_tensor(np.eye(128, dtype=np.float32), "ident_c")
    identb_c = nc.inline_tensor(np.eye(128, dtype=ml_dtypes.bfloat16), "identb_c")
    sq_np = np.zeros((128, 8, 16), np.float32)
    for fc in range(8):
        sq_np[0:64, fc, 2 * fc] = 1.0
        sq_np[64:128, fc, 2 * fc + 1] = 1.0
    selq_c = nc.inline_tensor(_bfc(sq_np.reshape(128, 128)), "selq_c")
    sk_np = np.zeros((128, 4, 8), np.float32)
    for fc in range(4):
        sk_np[0:64, fc, 2 * fc] = 1.0
        sk_np[64:128, fc, 2 * fc + 1] = 1.0
    selk_c = nc.inline_tensor(_bfc(sk_np.reshape(128, 32)), "selk_c")
    onesr_c = nc.inline_tensor(_bfc(np.ones((1, 128))), "onesr_c")
    ones4_c = nc.inline_tensor(_bfc(np.ones((1, 512))), "ones4_c")
    epsr_c = nc.inline_tensor(_bfc(np.full((1, 128), KEPS)), "epsr_c")
    onesc_c = nc.inline_tensor(_bfc(np.ones((128, 1))), "onesc_c")
    onesf_c = nc.inline_tensor(np.ones((1, 128), np.float32), "onesf_c")
    zbf_c = nc.inline_tensor(_bfc(np.zeros((1, 96))), "zbf_c")

    with ExitStack() as ctx:
        en = ctx.enter_context
        assert nc.SBUF_PARTITION_SIZE_BYTES >= OB + 212832, nc.SBUF_PARTITION_SIZE_BYTES

        R0 = Arena(nc, 0, 65536)
        xa = R0.at("xa", [128, NCH, E], F32)
        ln2a = Arena(nc, 0, 65536).at("ln2a", [128, NCH, E], F32)
        outa = Arena(nc, 0, 65536).at("outa", [128, NCH, E], F32)

        ln1T = Arena(nc, 65536, 131072).at("ln1T", [128, 8, NSEQ], BF16)
        eqT = Arena(nc, 65536, 131072).at("eqT", [128, 8, 2, NT], BF16)
        xmid = Arena(nc, 65536, 131072).at("xmid", [128, NCH, E], F32)

        r2s1 = Arena(nc, 131072, 163840)
        qrot = r2s1.at("qrot", [128, 4, 512], BF16)
        sqrot = r2s1.at("sqrot", [128, 4, 512], BF16)
        dgrot = r2s1.at("dgrot", [128, 4, 512], F32)
        vrot = r2s1.at("vrot", [128, 2, 512], BF16)
        edk_sb = r2s1.at("edk", [128, KCH, 8], F32)
        vss = r2s1.at("vss", [1, 512], F32)
        zrow = r2s1.at("zrow", [1, 256], F32)
        gx = Arena(nc, 131072, 139264).at("gx", [128, 2048], F32)
        r2s2 = Arena(nc, 139264, 147456)
        vtrot = r2s2.at("vtrot", [128, 2, 512], BF16)
        vvrot = r2s2.at("vvrot", [128, 2, 8, 66], BF16)
        kmaxt = r2s2.at("kmaxt", [128, 512], BF16)
        ekrot = Arena(nc, 155648, 163840).at("ekrot", [128, 2, 8, M], BF16)
        gf = Arena(nc, 131072, 147456).at("gf", [128, 4096], F32)
        r2s4a = Arena(nc, 147456, 155648)
        vsr = r2s4a.at("vsr", [1, 1024], F32)
        dq_all = r2s4a.at("dq_all", [128, NCH, 16], F32)
        ep1 = r2s4a.at("ep1", [128, 8], F32)
        ep2 = r2s4a.at("ep2", [128, 8], F32)
        ep3 = r2s4a.at("ep3", [128, 8], F32)
        c1s = r2s4a.at("c1s", [128, 512], F32)
        eqrot = Arena(nc, 155648, 163840).at("eqrot", [128, 1, NCH, M], BF16)
        r2pc = Arena(nc, 131072, 147456)
        vs2e = r2pc.at("vs2e", [128, 16, D], BF16)
        csum = r2pc.at("csum", [1, 16, 66], BF16)
        kse = r2pc.at("kse", [128, 16], F32)
        encol = r2pc.at("encol", [128, 1], F32)
        att_rot = r2pc.at("att_rot", [128, 2, 512], BF16)
        r2s5 = Arena(nc, 131072, 155648)
        woTs = r2s5.at("woTs", [128, 8, E], BF16)
        bob2 = r2s5.at("bob2", [1, E], BF16)
        brf5 = r2s5.at("brf5", [1, E], F32)
        bolo = r2s5.at("bolo", [1, E], BF16)
        r2s6 = Arena(nc, 131072, 163840)
        w8s = r2s6.at("w8s", [128, 8, E], I8)
        r2s6b = Arena(nc, 131072, 139264)
        qw = r2s6b.at("qw", [128, E], BF16)
        qsg = r2s6b.at("qsg", [128, E], BF16)
        wfbf = r2s6.at("wfbf", [128, 8, E], BF16)
        i8rot = r2s6.at("i8rot", [128, 2, E], I8)
        qtrot = r2s6.at("qtrot", [128, 2, E], BF16)
        fbp = r2s6.at("fbp", [1, E], BF16)

        kp = Arena(nc, 163840, 196608).at("kp", [128, 4, NSEQ], BF16)
        attn_fm = Arena(nc, 163840, 196608).at("attn_fm", [128, 8, NT], BF16)

        r4s0 = Arena(nc, 196608, 209024)
        xprot = r4s0.at("xprot", [128, 2, E], F32)
        ltrot = r4s0.at("ltrot", [128, 2, E], BF16)
        wq0 = Arena(nc, 196608, 204800).at("wq0", [128, 8, 512], BF16)
        wk = Arena(nc, 196608, 204800).at("wk", [128, 8, 512], BF16)
        wv = Arena(nc, 196608, 204800).at("wv", [128, 8, 512], BF16)
        qprot = Arena(nc, 196608, 204800).at("qprot", [128, 2, NT], BF16)
        caug = Arena(nc, 204800, 209024).at("caug", [128, 2, 16, 66], BF16)
        r4bq = Arena(nc, 204800, 209024)
        bqb = r4bq.at("bqb", [1, E], BF16)
        bqlo = r4bq.at("bqlo", [1, E], BF16)
        r4bk = Arena(nc, 204800, 209024)
        bkb = r4bk.at("bkb", [1, 512], BF16)
        bklo = r4bk.at("bklo", [1, 512], BF16)
        bvb2 = r4bk.at("bvb2", [1, 512], BF16)
        bvlo = r4bk.at("bvlo", [1, 512], BF16)
        brf = Arena(nc, 147456, 155648).at("brf", [1, E], F32)
        ones4 = Arena(nc, 153600, 155648).at("ones4", [1, 512], BF16)
        r4s6 = Arena(nc, 196608, 204800)
        wsbc = r4s6.at("wsbc", [128, E], F32)
        wsr = r4s6.at("wsr", [1, E], F32)
        fbrow = Arena(nc, 204800, 209024).at("fbrow", [1, E], F32)

        pm = Arena(nc, 209024, 212832)
        ident = pm.at("ident", [128, 128], F32)
        identb = Arena(nc, 154752, 155648).at("identb", [128, 128], BF16)
        identb2 = Arena(nc, 135168, 139264).at("identb2", [128, 128], BF16)
        selq = pm.at("selq", [128, 128], BF16)
        selk = pm.at("selk", [128, 32], BF16)
        onesr = pm.at("onesr", [1, 128], BF16)
        epsr = pm.at("epsr", [1, 128], BF16)
        onesc = pm.at("onesc", [128, 1], BF16)
        onesf = pm.at("onesf", [1, 128], F32)
        zbf = pm.at("zbf", [1, 96], BF16)
        projT = pm.at("projT", [128, 2 * M], BF16)
        stat = pm.at("stat", [128, 16], F32)
        scm = pm.at("scm", [128, 4], F32)
        stat2 = pm.at("stat2", [128, 8], F32)
        sc1 = pm.at("sc1", [128, 2], F32)
        sc2 = pm.at("sc2", [128, 2], F32)
        sc3 = pm.at("sc3", [128, 16], F32)
        srow = pm.at("srow", [1, 64], F32)
        jrow = pm.at("jrow", [1, 8], F32)

        psA = en(nc.psum_tensor("psA", [128, 4096], F32))
        psAb = psA.bitcast(BF16)

        glp = DPool(ctx, nc, "gl", 4)
        slp = DPool(ctx, nc, "sl", 4)
        gsp = DPool(ctx, nc, "gs", 4)
        tm = Sem(ctx, nc, "tm")
        ve = Sem(ctx, nc, "ve")
        se = Sem(ctx, nc, "se")
        cc = Sem(ctx, nc, "cc")
        blk = en(nc.Block())
        rec = {}

        Gb, Tb, Vb, Sb, Yb = blk.gpsimd, blk.tensor, blk.vector, blk.scalar, blk.sync

        def _stop(label):
            if _STOP != label:
                return False

            def f_stop(g):
                gsp.begin(g)
                yv = y_out.rearrange("(c p) f -> c p f", p=128)
                for n1 in range(16):
                    gsp.dma(g, yv[n1], outa[:, n1, :])
            Gb(f_stop)

            def f_sfin(g):
                gsp.wait_all(g)
            Gb(f_sfin)
            return True

        def c_consts(g):
            glp.begin(g)
            for dst, src in ((ident, ident_c), (selq, selq_c), (selk, selk_c),
                             (onesr, onesr_c), (epsr, epsr_c), (onesc, onesc_c),
                             (onesf, onesf_c)):
                glp.dma(g, dst[:], src.ap())
            glp.dma(g, projT[:], projTd)
            glp.dma(g, zbf[:], zbf_c.ap())
            glp.dma(g, identb[:], identb_c.ap())
            glp.dma(g, ones4[:], ones4_c.ap())
            rec["consts"] = glp.tok()
        Gb(c_consts)

        xv = x_own.rearrange("(c p) f -> c p f", p=128)
        pv = x_par.rearrange("(c p) f -> c p f", p=128)

        for c in range(KCH):
            def f_load(g, c=c):
                if c >= NCH and c - NCH >= 2:
                    ve.wait(g, rec[("apl", c - 2)])
                glp.begin(g)
                if c < NCH:
                    glp.dma(g, xa[:, c, :], xv[c])
                else:
                    glp.dma(g, xprot[:, c % 2, :], pv[c - NCH])
                rec[("xl", c)] = glp.tok()
            Gb(f_load)

            if c % 2 == 0:
                continue
            c0, c1 = c - 1, c
            j0, j1 = c0 % 4, c1 % 4
            src0 = xa[:, c0, :] if c0 < NCH else xprot[:, c0 % 2, :]
            src1 = xa[:, c1, :] if c1 < NCH else xprot[:, c1 % 2, :]

            def f_v1(v, src0=src0, src1=src1, j0=j0, j1=j1, c0=c0, c1=c1):
                if c0 == 0:
                    v.memset(jrow[:], 0.0)
                tw(v, rec[("xl", c0)])
                tw(v, rec[("xl", c1)])
                v.tensor_reduce(stat[:, j0: j0 + 1], src0, AX.X, ALU.add)
                v.tensor_reduce(stat[:, j1: j1 + 1], src1, AX.X, ALU.add)
                v.tensor_scalar(stat[:, 4 + j0: 5 + j0], stat[:, j0: j0 + 1], 1.0 / E, None, ALU.mult)
                v.tensor_scalar(stat[:, 4 + j1: 5 + j1], stat[:, j1: j1 + 1], 1.0 / E, None, ALU.mult)
                v.tensor_tensor(stat2[:, j0: j0 + 1], stat[:, 4 + j0: 5 + j0], stat[:, 4 + j0: 5 + j0], ALU.mult)
                v.tensor_tensor(stat2[:, j1: j1 + 1], stat[:, 4 + j1: 5 + j1], stat[:, 4 + j1: 5 + j1], ALU.mult)
                v.tensor_scalar(stat2[:, j0: j0 + 1], stat2[:, j0: j0 + 1], -1.0, LNEPS,
                                ALU.mult, ALU.add)
                ve.inc(v.tensor_scalar(stat2[:, j1: j1 + 1], stat2[:, j1: j1 + 1], -1.0, LNEPS,
                                       ALU.mult, ALU.add))
            Vb(f_v1)

            def f_s1(s, src0=src0, src1=src1, j0=j0, j1=j1, c0=c0, c1=c1):
                tw(s, rec[("xl", c0)])
                tw(s, rec[("xl", c1)])
                if c0 >= 2:
                    ve.wait(s, rec[("ltc", c1 - 2)])
                s.activation(ltrot[:, 0, :], src0, AF.Square, accum_out=scm[:, j0: j0 + 1])
                s.activation(ltrot[:, 1, :], src1, AF.Square, accum_out=scm[:, j1: j1 + 1])
                ve.wait(s)
                s.activation(sc1[:, 0:1], scm[:, j0: j0 + 1], AF.Sqrt,
                             scale=1.0 / E, bias=stat2[:, j0: j0 + 1])
                se.inc(s.activation(sc1[:, 1:2], scm[:, j1: j1 + 1], AF.Sqrt,
                                    scale=1.0 / E, bias=stat2[:, j1: j1 + 1]))
            Sb(f_s1)

            def f_v2(v, j0=j0, j1=j1):
                se.wait(v)
                v.reciprocal(sc2[:, 0:1], sc1[:, 0:1])
                v.reciprocal(sc2[:, 1:2], sc1[:, 1:2])
                v.tensor_scalar(sc3[:, j0: j0 + 1], stat[:, 4 + j0: 5 + j0], sc2[:, 0:1],
                                -1.0, ALU.mult, ALU.mult)
                ve.inc(v.tensor_scalar(sc3[:, j1: j1 + 1], stat[:, 4 + j1: 5 + j1], sc2[:, 1:2],
                                       -1.0, ALU.mult, ALU.mult))
            Vb(f_v2)

            for ci, (cn, jj, srcc) in enumerate(((c0, j0, src0), (c1, j1, src1))):
                def f_s2(v, srcc=srcc, jj=jj, cn=cn, ci=ci):
                    rec[("apl", cn)] = ve.n + 1
                    ve.inc(v.tensor_scalar(ltrot[:, ci, :], srcc, sc2[:, ci: ci + 1],
                                           sc3[:, jj: jj + 1], ALU.mult, ALU.add))
                Vb(f_s2)

                def f_yt(t, cn=cn, ci=ci):
                    ve.wait(t, rec[("apl", cn)])
                    if cn >= 2:
                        ve.wait(t, rec[("ltc", cn - 2)])
                    base = 4096 + 1024 * (cn % 2)
                    for ech in range(8):
                        mi_auto = t.transpose(
                            psAb[:, base + 128 * ech: base + 128 * (ech + 1)],
                            ltrot[:, ci, 128 * ech: 128 * (ech + 1)], identb[:])
                    tm.inc(mi_auto)
                Tb(f_yt)

                def f_ye(v, cn=cn):
                    tm.wait(v)
                    base = 4096 + 1024 * (cn % 2)
                    for ech in range(8):
                        mi_auto = v.tensor_copy(
                            ln1T[:, ech, 128 * cn: 128 * (cn + 1)],
                            psAb[:, base + 128 * ech: base + 128 * (ech + 1)])
                    rec[("ltc", cn)] = ve.n + 1
                    ve.inc(mi_auto)
                Vb(f_ye)

        if "ln1T" in tap:
            def t_ln1T(g):
                slp.wait_all(g)
                gsp.begin(g)
                for ech in range(8):
                    gsp.dma(g, tap["ln1T"][ech], ln1T[:, ech, :])
            Gb(t_ln1T)

        if _stop("s0"):
            return nc, tap

        wqp = wqT.rearrange("(ec p) f -> p ec f", p=128)
        wkp = wkT.rearrange("(ec p) f -> p ec f", p=128)
        wvp = wvT.rearrange("(ec p) f -> p ec f", p=128)

        def f_wq0(g):
            ve.wait(g)
            ve.wait(g, rec[("ltc", KCH - 1)])
            glp.begin(g)
            glp.dma(g, wq0[:, :, :], wqp[:, :, 0:512])
            glp.dma(g, brf[:], bq_row)
            rec["wq0"] = glp.tok()
        Gb(f_wq0)

        def f_bqprep(v):
            tw(v, rec["wq0"])
            v.tensor_copy(bqb[:], brf[:])
            v.tensor_scalar(jrow[:], jrow[:], 0.0, None, ALU.mult)
            ve.inc(v.tensor_tensor(bqlo[:], brf[:], bqb[:], ALU.subtract))
        Vb(f_bqprep)

        if _stop("s1w"):
            return nc, tap

        for fch in range(8):
            fh, fi = fch // 4, fch % 4
            if fch == 4:
                def f_wq1(g):
                    tm.wait(g)
                    glp.begin(g)
                    glp.dma(g, wq0[:, :, :], wqp[:, :, 512:1024])
                    rec["wq1"] = glp.tok()
                Gb(f_wq1)

            def f_qmm(t, fch=fch, fh=fh, fi=fi):
                tw(t, rec["wq1"] if fh else rec["wq0"])
                tw(t, rec["consts"])
                if fch == 0:
                    ve.wait(t)
                slp.wait_all(t)
                if fch > 0:
                    se.wait(t, rec[("qev", fch - 1)])
                    ve.wait(t, rec[("qcv", fch - 1)])
                for ech in range(8):
                    for ti in range(4):
                        t.matmul(psA[:, 512 * ti: 512 * (ti + 1)],
                                 wq0[:, ech, 128 * fi: 128 * (fi + 1)],
                                 ln1T[:, ech, 512 * ti: 512 * (ti + 1)],
                                 start=(ech == 0), stop=False)
                for ti in range(4):
                    t.matmul(psA[:, 512 * ti: 512 * (ti + 1)],
                             bqb[0:1, 128 * fch: 128 * (fch + 1)], ones4[0:1, :],
                             start=False, stop=False)
                    mi_auto = t.matmul(psA[:, 512 * ti: 512 * (ti + 1)],
                             bqlo[0:1, 128 * fch: 128 * (fch + 1)], ones4[0:1, :],
                             start=False, stop=True)
                tm.inc(mi_auto)
            Tb(f_qmm)

            def f_qcv(v, fch=fch):
                tm.wait(v)
                if fch > 0:
                    tw(v, rec[("qst", fch - 1)])
                for ti in range(4):
                    mi_auto = v.tensor_scalar(qrot[:, ti, :],
                                              psA[:, 512 * ti: 512 * (ti + 1)],
                                              S2C, None, ALU.mult)
                rec[("qcv", fch)] = ve.n + 1
                ve.inc(mi_auto)
            Vb(f_qcv)

            def f_qev(s, fch=fch):
                tm.wait(s)
                for ti in range(4):
                    mi_auto = s.activation(sqrot[:, ti, :], psA[:, 512 * ti: 512 * (ti + 1)],
                                 AF.Square, scale=S2C)
                rec[("qev", fch)] = se.n + 1
                se.inc(mi_auto)
            Sb(f_qev)

            def f_qst(g, fch=fch):
                ve.wait(g, rec[("qcv", fch)])
                gsp.begin(g)
                for ti in range(4):
                    gsp.dma(g, d_qp[fch][:, 512 * ti: 512 * (ti + 1)], qrot[:, ti, :])
                rec[("qst", fch)] = gsp.tok()
            Gb(f_qst)

            def f_qdg(t, fch=fch):
                se.wait(t)
                for ti in range(4):
                    mi_auto = t.matmul(psA[0:16, 2048 + 512 * ti: 2048 + 512 * (ti + 1)],
                             selq[:, 16 * fch: 16 * (fch + 1)], sqrot[:, ti, :],
                             start=(fch == 0), stop=(fch == 7), skip_group_check=True)
                tm.inc(mi_auto)
            Tb(f_qdg)

        if _stop("s1q"):
            return nc, tap
        def f_dqev(s):
            tm.wait(s)
            for ti in range(4):
                mi_auto = s.copy(dgrot[0:16, ti, :], psA[0:16, 2048 + 512 * ti: 2048 + 512 * (ti + 1)])
            se.inc(mi_auto)
        Sb(f_dqev)

        def f_dqtr(t):
            se.wait(t)
            for ti in range(4):
                for sub in range(4):
                    mi_auto = t.transpose(psA[:, 512 * ti + 16 * sub: 512 * ti + 16 * (sub + 1)],
                                dgrot[0:16, ti, 128 * sub: 128 * (sub + 1)], ident[0:16, 0:16])
            tm.inc(mi_auto)
        Tb(f_dqtr)

        def f_dqev2(s):
            tm.wait(s)
            for ti in range(4):
                for sub in range(4):
                    mi_auto = s.activation(dgrot[:, ti, 16 * sub: 16 * (sub + 1)],
                                 psA[:, 512 * ti + 16 * sub: 512 * ti + 16 * (sub + 1)],
                                 AF.Copy, scale=0.5)
            se.inc(mi_auto)
        Sb(f_dqev2)

        def f_dqst(g):
            se.wait(g)
            gsp.begin(g)
            for ti in range(4):
                for sub in range(4):
                    gsp.dma(g, d_dqT[4 * ti + sub], dgrot[:, ti, 16 * sub: 16 * (sub + 1)])
            rec["dqst"] = gsp.tok()
        Gb(f_dqst)

        if _stop("s1dq"):
            return nc, tap

        def f_wk(g):
            tm.wait(g)
            glp.begin(g)
            glp.dma(g, wk[:, :, :], wkp)
            glp.dma(g, brf[0:1, 0:512], bk_row)
            glp.dma(g, brf[0:1, 512:1024], bv_row)
            rec["wk"] = glp.tok()
        Gb(f_wk)

        def f_bkprep(v):
            tw(v, rec["wk"])
            v.tensor_copy(bkb[:], brf[0:1, 0:512])
            v.tensor_copy(bvb2[:], brf[0:1, 512:1024])
            v.tensor_tensor(bklo[:], brf[0:1, 0:512], bkb[:], ALU.subtract)
            ve.inc(v.tensor_tensor(bvlo[:], brf[0:1, 512:1024], bvb2[:], ALU.subtract))
        Vb(f_bkprep)

        for grp in range(2):
            for fch in range(4):
                def f_kmm(t, grp=grp, fch=fch):
                    tw(t, rec["wk"])
                    if grp == 0 and fch == 0:
                        ve.wait(t)
                    if fch:
                        se.wait(t, rec[("kev", grp, fch - 1)])
                        ve.wait(t, rec[("kcv", grp, fch - 1)])
                    else:
                        se.wait(t)
                        ve.wait(t)
                    for ech in range(8):
                        for ti in range(4):
                            col = 2048 * grp + 512 * ti
                            t.matmul(psA[:, 512 * ti: 512 * (ti + 1)],
                                     wk[:, ech, 128 * fch: 128 * (fch + 1)],
                                     ln1T[:, ech, col: col + 512],
                                     start=(ech == 0), stop=False)
                    for ti in range(4):
                        t.matmul(psA[:, 512 * ti: 512 * (ti + 1)],
                                 bkb[0:1, 128 * fch: 128 * (fch + 1)], ones4[0:1, :],
                                 start=False, stop=False)
                        mi_auto = t.matmul(psA[:, 512 * ti: 512 * (ti + 1)],
                                 bklo[0:1, 128 * fch: 128 * (fch + 1)], ones4[0:1, :],
                                 start=False, stop=True)
                    tm.inc(mi_auto)
                Tb(f_kmm)

                def f_kcv(v, grp=grp, fch=fch):
                    tm.wait(v)
                    for ti in range(4):
                        col = 2048 * grp + 512 * ti
                        mi_auto = v.tensor_scalar(kp[:, fch, col: col + 512],
                                                  psA[:, 512 * ti: 512 * (ti + 1)],
                                                  S2C, None, ALU.mult)
                    rec[("kcv", grp, fch)] = ve.n + 1
                    ve.inc(mi_auto)
                Vb(f_kcv)

                def f_kev(s, grp=grp, fch=fch):
                    tm.wait(s)
                    for ti in range(4):
                        mi_auto = s.activation(sqrot[:, ti, :], psA[:, 512 * ti: 512 * (ti + 1)],
                                     AF.Square, scale=S2C)
                    rec[("kev", grp, fch)] = se.n + 1
                    se.inc(mi_auto)
                Sb(f_kev)

                def f_kdg(t, fch=fch):
                    se.wait(t)
                    for ti in range(4):
                        mi_auto = t.matmul(psA[0:8, 2048 + 512 * ti: 2048 + 512 * ti + 512],
                                 selk[:, 8 * fch: 8 * (fch + 1)], sqrot[:, ti, :],
                                 start=(fch == 0), stop=(fch == 3), skip_group_check=True)
                    tm.inc(mi_auto)
                Tb(f_kdg)

            def f_kdev(s):
                tm.wait(s)
                tw(s, rec["dqst"])
                for ti in range(4):
                    mi_auto = s.copy(dgrot[0:8, ti, :], psA[0:8, 2048 + 512 * ti: 2048 + 512 * ti + 512])
                se.inc(mi_auto)
            Sb(f_kdev)

            def f_kdtr(t):
                se.wait(t)
                for ti in range(4):
                    for sub in range(4):
                        mi_auto = t.transpose(psA[:, 512 * ti + 16 * sub: 512 * ti + 16 * sub + 8],
                                    dgrot[0:8, ti, 128 * sub: 128 * (sub + 1)], ident[0:8, 0:8])
                tm.inc(mi_auto)
            Tb(f_kdtr)

            def f_kded(s, grp=grp):
                tm.wait(s)
                for ti in range(4):
                    for sub in range(4):
                        kc = 16 * grp + 4 * ti + sub
                        mi_auto = s.activation(edk_sb[:, kc, :],
                                     psA[:, 512 * ti + 16 * sub: 512 * ti + 16 * sub + 8],
                                     AF.Exp, scale=-0.5)
                se.inc(mi_auto)
            Sb(f_kded)

        if _stop("s1k"):
            return nc, tap

        def f_wv(g):
            tm.wait(g)
            glp.begin(g)
            glp.dma(g, wv[:, :, :], wvp)
            rec["wv"] = glp.tok()
        Gb(f_wv)

        for n2 in range(KCH):
            def f_vmm(t, n2=n2):
                tw(t, rec["wv"])
                if n2 == 0:
                    se.wait(t)
                    ve.wait(t)
                if n2 >= 2:
                    ve.wait(t, rec[("vev", n2 - 2)])
                bi = n2 % 2
                for ech in range(8):
                    t.matmul(psA[:, 512 * bi: 512 * (bi + 1)],
                             ln1T[:, ech, 128 * n2: 128 * (n2 + 1)],
                             wv[:, ech, :], start=(ech == 0), stop=False)
                t.matmul(psA[:, 512 * bi: 512 * (bi + 1)],
                         onesr[0:1, 0:128], bvb2[0:1, :], start=False, stop=False)
                mi_auto = t.matmul(psA[:, 512 * bi: 512 * (bi + 1)],
                         onesr[0:1, 0:128], bvlo[0:1, :], start=False, stop=True)
                tm.inc(mi_auto)
            Tb(f_vmm)

            def f_vev(v, n2=n2):
                tm.wait(v)
                if n2 >= 2:
                    tw(v, rec[("vst", n2 - 2)])
                rec[("vev", n2)] = ve.n + 1
                ve.inc(v.tensor_copy(vrot[:, n2 % 2, :],
                                     psA[:, 512 * (n2 % 2): 512 * (n2 % 2) + 512]))
            Vb(f_vev)

            def f_vs(t, n2=n2):
                ve.wait(t)
                mi_auto = t.matmul(psA[0:1, 1024:1536], onesc[:, 0:1], vrot[:, n2 % 2, :],
                         start=(n2 == 0), stop=(n2 == KCH - 1), skip_group_check=True)
                tm.inc(mi_auto)
            Tb(f_vs)

            def f_vst(g, n2=n2):
                ve.wait(g)
                gsp.begin(g)
                gsp.dma(g, d_vtok[n2], vrot[:, n2 % 2, :])
                rec[("vst", n2)] = gsp.tok()
            Gb(f_vst)

        def f_vsev(v):
            tm.wait(v)
            ve.inc(v.tensor_copy(vss[0:1, :], psA[0:1, 1024:1536]))
        Vb(f_vsev)

        if _stop("s1v"):
            return nc, tap

        for tn, tsrc, cnt in (("qp", d_qp, 8), ("vtok", d_vtok, KCH), ("dqT", d_dqT, NCH)):
            if tn in tap:
                def t_d(g, tn=tn, tsrc=tsrc, cnt=cnt):
                    gsp.wait_all(g)
                    gsp.begin(g)
                    for i in range(cnt):
                        gsp.dma(g, tap[tn][i], tsrc[i])
                Gb(t_d)
        if "kp" in tap:
            def t_kp(g):
                se.wait(g)
                gsp.begin(g)
                for i in range(4):
                    gsp.dma(g, tap["kp"][i], kp[:, i, :])
            Gb(t_kp)
        if "edk" in tap:
            def t_edk(g):
                se.wait(g)
                gsp.begin(g)
                gsp.dma(g, tap["edk"][:], edk_sb[:])
            Gb(t_edk)

        if _stop("s1"):
            return nc, tap

        def f_km0(v):
            tm.wait(v)
            ve.inc(v.memset(kmaxt[:], 0.0))
        Vb(f_km0)

        def f_gclr(t):
            se.wait(t)
            ve.wait(t)
            tw(t, rec["consts"])
            for b2 in range(4):
                mi_auto = t.matmul(psA[0:65, 512 * b2: 512 * (b2 + 1)], zbf[0:1, 0:65],
                         ones4[0:1, :], start=True, stop=False, skip_group_check=True)
            tm.inc(mi_auto)
        Tb(f_gclr)

        _kcn = {"s2a0": 0, "s2a1": 1, "s2a2": 1, "s2a": 2, "s2v1": 1, "s2v2": 1,
                "s2v3": 1, "s2v4": 1, "s2w1": 1, "s2w2": 1}.get(_STOP, KCH)
        _s2l = {"s2v1": 1, "s2v2": 2, "s2v3": 3, "s2v4": 4,
                "s2w1": 3, "s2w2": 3}.get(_STOP, 9)
        for kc in range(_kcn):
            def f_vtl(g, kc=kc):
                tw(g, rec[("vst", kc)])
                if kc == 0:
                    tm.wait(g)
                if kc >= 2:
                    ve.wait(g, rec[("vvb", kc - 2)])
                glp.begin(g)
                glp.dma(g, vtrot[:, kc % 2, :], d_vtok[kc])
                rec[("vtl", kc)] = glp.tok()
            Gb(f_vtl)

            def f_vvb(v, kc=kc):
                tw(v, rec[("vtl", kc)])
                if kc == 0:
                    se.wait(v)
                if kc >= 2:
                    tm.wait(v, rec[("gmm", kc - 2)])
                s = kc % 2
                for h in range(8):
                    v.tensor_scalar(vvrot[:, s, h, 0:64], vtrot[:, s, 64 * h: 64 * (h + 1)],
                                    edk_sb[:, kc, h: h + 1], None, ALU.mult)
                rec[("vvb", kc)] = ve.n + 1
                ve.inc(v.tensor_copy(
                    vvrot[:, s, :, 64:65].rearrange("p a b -> p (a b)"),
                    edk_sb[:, kc, :]))
            if _s2l < 2:
                continue
            Vb(f_vvb)

            for half in range(2):
                def f_dd(t, kc=kc, half=half):
                    if kc == 0:
                        se.wait(t)
                        ve.wait(t)
                    if kc > 0:
                        se.wait(t, rec[("exp", kc - 1, half)])
                    for fch in (2 * half, 2 * half + 1):
                        base = 2048 + 1024 * half + 512 * (fch % 2)
                        mi_auto = t.matmul(psA[:, base: base + 512],
                                 kp[:, fch, 128 * kc: 128 * (kc + 1)],
                                 projT[:, :], start=True, stop=True)
                    tm.inc(mi_auto)
                if _s2l < 3:
                    continue
                Tb(f_dd)

                def f_exp(s, kc=kc, half=half):
                    tm.wait(s)
                    if kc >= 2:
                        ve.wait(s, rec[("kmx", kc - 2)])
                        tm.wait(s, rec[("gmm", kc - 2)])
                    rec[("exp", kc, half)] = se.n + 1
                    se.inc(s.activation(
                        ekrot[:, kc % 2, 4 * half: 4 * half + 4, :].rearrange("p a b -> p (a b)"),
                        psA[:, 2048 + 1024 * half: 2048 + 1024 * (half + 1)],
                        AF.Exp))
                if _s2l < 4:
                    continue
                Sb(f_exp)

            def f_kmx(v, kc=kc):
                se.wait(v)
                s = kc % 2
                for i in range(4):
                    mi_auto = v.tensor_tensor(kmaxt[:], kmaxt[:],
                                    ekrot[:, s, 2 * i: 2 * i + 2, :].rearrange("p a b -> p (a b)"),
                                    ALU.max)
                rec[("kmx", kc)] = ve.n + 1
                ve.inc(mi_auto)
            if _s2l < 5:
                continue
            Vb(f_kmx)

            def f_g(t, kc=kc):
                ve.wait(t)
                for h in range(8):
                    mi_auto = t.matmul(psA[0:65, 256 * h: 256 * (h + 1)],
                             vvrot[:, kc % 2, h, 0:65],
                             ekrot[:, kc % 2, h, :],
                             start=False, stop=(kc == KCH - 1), skip_group_check=True)
                rec[("gmm", kc)] = tm.n + 1
                tm.inc(mi_auto)
            if _STOP != "s2a1":
                Tb(f_g)

        if _STOP in ("s2a0", "s2a1", "s2v1", "s2v2", "s2v3", "s2v4",
                     "s2w1", "s2w2"):
            def f_dbg2(g):
                tm.wait(g)
                ve.wait(g)
                se.wait(g)
            Gb(f_dbg2)
        if (_stop("s2a") or _stop("s2b") or _stop("s2a0") or _stop("s2a1")
                or _stop("s2a2") or _stop("s2v1") or _stop("s2v2") or _stop("s2v3")
                or _stop("s2v4") or _stop("s2w1") or _stop("s2w2")):
            return nc, tap

        def f_gev(s):
            tm.wait(s)
            for h in range(8):
                mi_auto = s.copy(gx[0:65, 256 * h: 256 * (h + 1)], psA[0:65, 256 * h: 256 * (h + 1)])
            se.inc(mi_auto)
        Sb(f_gev)

        if _stop("s2c"):
            return nc, tap

        def f_kmf1(v):
            ve.wait(v)
            ve.inc(v.tensor_reduce(sc1[:, 0:1], kmaxt[:], AX.X, ALU.max))
        Vb(f_kmf1)

        def f_kmf2(t):
            ve.wait(t)
            se.wait(t)
            mi_auto = t.transpose(psA[0:1, 2048:2176], sc1[:, 0:1], ident[:])
            tm.inc(mi_auto)
        Tb(f_kmf2)

        def f_kmf3(v):
            tm.wait(v)
            mi_auto = v.tensor_reduce(srow[0:1, 0:1], psA[0:1, 2048:2176], AX.X, ALU.max)
            ve.inc(mi_auto)
        Vb(f_kmf3)

        def f_kmf4(s):
            ve.wait(s)
            se.inc(s.activation(srow[0:1, 8:9], srow[0:1, 0:1], AF.Ln))
        Sb(f_kmf4)

        def f_kmf5(v):
            se.wait(v)
            for i in range(1, 8):
                v.tensor_copy(srow[0:1, 8 + i: 9 + i], srow[0:1, 8:9])
            mi_auto = v.memset(zrow[:], 0.0)
            ve.inc(mi_auto)
        Vb(f_kmf5)

        if _stop("s2"):
            return nc, tap

        def f_ag(g):
            se.wait(g)
            ve.wait(g)
            gsp.begin(g)
            for h in range(8):
                gsp.dma(g, ag_in.ap()[h, 0:65, :], gx[0:65, 256 * h: 256 * (h + 1)])
                gsp.dma(g, ag_in.ap()[h, 65:66, 0:64], vss[0:1, 64 * h: 64 * (h + 1)])
                gsp.dma(g, ag_in.ap()[h, 65:66, 64:256], zrow[0:1, 0:192])
            gsp.dma(g, mxk_in.ap()[:, :], srow[0:1, 8:16])
            tw(g, gsp.tok())
            cc.inc(g.collective_compute(
                "AllGather", ALU.bypass,
                replica_groups=[[0, 1], [2, 3], [4, 5], [6, 7]],
                ins=[ag_in.ap().opt()], outs=[ag_out.ap().opt()]))
            cc.inc(g.collective_compute(
                "AllReduce", ALU.max,
                replica_groups=[[0, 1, 2, 3, 4, 5, 6, 7]],
                ins=[mxk_in.ap().opt()], outs=[mxk_out.ap().opt()]))
            g.wait_ge(cc.h, cc.n)
            glp.begin(g)
            for h in range(16):
                glp.dma(g, gf[0:65, 256 * h: 256 * (h + 1)], ag_out.ap()[h, 0:65, :])
                glp.dma(g, vsr[0:1, 64 * h: 64 * (h + 1)], ag_out.ap()[h, 65:66, 0:64])
            glp.dma(g, srow[0:1, 16:24], mxk_out.ap()[:, :])
            rec["ex_done"] = glp.tok()
        Gb(f_ag)

        if "gx" in tap:
            def t_gx(g):
                cc.wait(g)
                gsp.begin(g)
                for h in range(8):
                    gsp.dma(g, tap["gx"][h], ag_in.ap()[h])
            Gb(t_gx)
        if "gf" in tap:
            def t_gf(g):
                cc.wait(g)
                gsp.begin(g)
                for h in range(16):
                    gsp.dma(g, tap["gf"][h], ag_out.ap()[h])
            Gb(t_gf)
        if "mxk" in tap:
            def t_mxk(g):
                cc.wait(g)
                gsp.begin(g)
                gsp.dma(g, tap["mxk"][:, :], mxk_out.ap()[:, :])
            Gb(t_mxk)

        if _stop("ex"):
            return nc, tap

        def f_dqld(g):
            se.wait(g)
            tw(g, rec["dqst"])
            ve.wait(g)
            glp.begin(g)
            glp.dma(g, dq_all[:, :, :], d_dqT.rearrange("c p j -> p c j"))
            rec["dqld"] = glp.tok()
        Gb(f_dqld)

        def emit_dd_half(half):
            for i in range(4):
                fc = 4 * half + i

                def f_qpl(g, fc=fc, i=i, half=half):
                    if half == 0 and i == 0:
                        tm.wait(g)
                    tw(g, rec[("qst", fc)])
                    if i >= 2 or half == 1:
                        prev = fc - 2
                        se.wait(g, rec[("eqx", prev)])
                    glp.begin(g)
                    glp.dma(g, qprot[:, fc % 2, :], d_qp[fc])
                    rec[("qpl", fc)] = glp.tok()
                Gb(f_qpl)

                for sub in range(2):
                    h = 2 * fc + sub
                    hl = h - 8 * half
                    for blkk in range(2):
                        def f_ddq(t, fc=fc, sub=sub, h=h, blkk=blkk):
                            tw(t, rec[("qpl", fc)])
                            tw(t, rec["dqld"])
                            if h % 8 == 0 and blkk == 0:
                                ve.wait(t)
                            se.wait(t)
                            for n1 in range(8 * blkk, 8 * blkk + 8):
                                slot = n1 % 8
                                mi_auto = t.matmul(psA[:, 256 * slot: 256 * (slot + 1)],
                                         qprot[64 * sub: 64 * sub + 64, fc % 2,
                                               128 * n1: 128 * (n1 + 1)],
                                         projT[64 * sub: 64 * sub + 64,
                                               256 * sub: 256 * sub + 256],
                                         start=True, stop=True)
                            rec[("ddq", h, blkk)] = tm.n + 1
                            tm.inc(mi_auto)
                        Tb(f_ddq)

                        def f_stab(v, h=h, blkk=blkk):
                            tm.wait(v, rec[("ddq", h, blkk)])
                            se.wait(v)
                            for n1 in range(8 * blkk, 8 * blkk + 8):
                                slot = n1 % 8
                                v.tensor_reduce(ep1[:, slot: slot + 1],
                                                psA[:, 256 * slot: 256 * (slot + 1)],
                                                AX.X, ALU.max, negate=True)
                            for n1 in range(8 * blkk, 8 * blkk + 8):
                                slot = n1 % 8
                                mi_auto = v.tensor_tensor(ep2[:, slot: slot + 1],
                                                ep1[:, slot: slot + 1],
                                                dq_all[:, n1, h: h + 1], ALU.subtract)
                            ve.inc(mi_auto)
                        Vb(f_stab)

                        def f_eqx(s, h=h, hl=hl, fc=fc, blkk=blkk, half=half):
                            ve.wait(s)
                            if hl == 0 and blkk == 0:
                                tm.wait(s)
                            if blkk == 0 and hl >= 1:
                                ve.wait(s, rec[("eqtr", hl - 1, half)])
                            for n1 in range(8 * blkk, 8 * blkk + 8):
                                slot = n1 % 8
                                mi_auto = s.activation(eqrot[:, 0, n1, :],
                                             psA[:, 256 * slot: 256 * (slot + 1)],
                                             AF.Exp, bias=ep2[:, slot: slot + 1])
                            rec[("eqx", fc)] = se.n + 1
                            se.inc(mi_auto)
                        Sb(f_eqx)

                    for wv in range(2):
                        par = (2 * hl + wv) % 2
                        def f_eqt_t(t, hl=hl, half=half, wv=wv, par=par):
                            se.wait(t)
                            if ("eqte", half, par) in rec:
                                ve.wait(t, rec[("eqte", half, par)])
                            base = 4096 + 2048 * par
                            k = 0
                            for n1 in range(8 * wv, 8 * wv + 8):
                                for mc in range(2):
                                    mi_auto = t.transpose(
                                        psAb[:, base + 128 * k: base + 128 * (k + 1)],
                                        eqrot[:, 0, n1, 128 * mc: 128 * (mc + 1)],
                                        identb[:])
                                    k += 1
                            tm.inc(mi_auto)
                        Tb(f_eqt_t)

                        def f_eqt_e(v, hl=hl, half=half, wv=wv, par=par):
                            tm.wait(v)
                            base = 4096 + 2048 * par
                            k = 0
                            for n1 in range(8 * wv, 8 * wv + 8):
                                for mc in range(2):
                                    mi_auto = v.tensor_copy(
                                        eqT[:, hl, mc, 128 * n1: 128 * (n1 + 1)],
                                        psAb[:, base + 128 * k: base + 128 * (k + 1)])
                                    k += 1
                            rec[("eqte", half, par)] = ve.n + 1
                            ve.inc(mi_auto)
                        Vb(f_eqt_e)
                    rec[("eqtr", hl, half)] = rec[("eqte", half, (2 * hl + 1) % 2)]

        emit_dd_half(0)

        if _stop("s4a"):
            return nc, tap

        def f_et(s):
            tw(s, rec["ex_done"])
            se.inc(s.activation(srow[0:1, 17:18], srow[0:1, 16:17], AF.Exp))
        Sb(f_et)

        def f_rows(v):
            se.wait(v)
            v.tensor_scalar(srow[0:1, 18:19], srow[0:1, 17:18],
                            float(KEPS) * float(NSEQ * KEPS * M), None, ALU.mult)
            v.tensor_scalar(srow[0:1, 19:20], srow[0:1, 17:18],
                            float(KEPS) * float(NSEQ), None, ALU.mult)
            mi_auto = v.tensor_scalar(srow[0:1, 17:18], srow[0:1, 17:18], KEPS, None, ALU.mult)
            ve.inc(mi_auto)
        Vb(f_rows)

        for h in range(16):
            for mc2 in range(2):
                i = 2 * h + mc2
                cb = 1536 + 512 * (i % 2)

                def f_ctr(t, h=h, mc2=mc2, i=i, cb=cb):
                    if i == 0:
                        tw(t, rec["ex_done"])
                        ve.wait(t)
                        se.wait(t)
                    if i >= 2:
                        ve.wait(t, rec[("cbe", i - 2)])
                    mi_auto = t.transpose(psA[:, cb: cb + 65],
                                gf[0:65, 256 * h + 128 * mc2: 256 * h + 128 * (mc2 + 1)],
                                ident[0:65, 0:65])
                    tm.inc(mi_auto)
                Tb(f_ctr)

                def f_cbe(v, h=h, mc2=mc2, i=i, cb=cb):
                    tm.wait(v)
                    v.tensor_copy(caug[:, mc2, h, 0:65], psA[:, cb: cb + 65])
                    mi_auto = v.memset(caug[:, mc2, h, 65:66], 1.0)
                    rec[("cbe", i)] = ve.n + 1
                    ve.inc(mi_auto)
                Vb(f_cbe)

        for h in range(16):
            def f_csmm(t, h=h):
                ve.wait(t)
                for mc2 in range(2):
                    mi_auto = t.matmul(psA[0:1, 1280:1346], onesc[:, 0:1], caug[:, mc2, h, :],
                             start=(mc2 == 0), stop=(mc2 == 1), skip_group_check=True)
                tm.inc(mi_auto)
            Tb(f_csmm)

            def f_csev(v, h=h):
                tm.wait(v)
                ve.inc(v.tensor_copy(csum[0:1, h, :], psA[0:1, 1280:1346]))
            Vb(f_csev)

        def f_kser(v):
            ve.wait(v)
            v.tensor_scalar(jrow[:], jrow[:], 0.0, None, ALU.mult)
            ve.inc(v.tensor_scalar(srow[0:1, 32:48],
                                   csum[0:1, :, 64:65].rearrange("p a b -> p (a b)"),
                                   KEPS, srow[0:1, 18:19], ALU.mult, ALU.add))
        Vb(f_kser)

        def f_bc(t):
            ve.wait(t)
            t.matmul(psA[:, 0:512], onesf[0:1, :], vsr[0:1, 0:512], start=True, stop=True)
            t.matmul(psA[:, 512:1024], onesf[0:1, :], vsr[0:1, 512:1024], start=True, stop=True)
            t.matmul(psA[:, 1024:1040], onesf[0:1, :], srow[0:1, 32:48], start=True, stop=True)
            t.matmul(psA[:, 1040:1041], onesf[0:1, :], srow[0:1, 19:20], start=True, stop=True)
            mi_auto = t.matmul(psA[:, 1041:1042], onesf[0:1, :], srow[0:1, 17:18], start=True, stop=True)
            tm.inc(mi_auto)
        Tb(f_bc)

        def f_bcev(v):
            tm.wait(v)
            vf = vs2e.ap().rearrange("p h d -> p (h d)")
            v.tensor_copy(scm[:, 2:3], psA[:, 1041:1042])
            v.tensor_copy(kse[:], psA[:, 1024:1040])
            v.tensor_scalar(vf[:, 0:512], psA[:, 0:512], scm[:, 2:3], None, ALU.mult)
            v.tensor_scalar(vf[:, 512:1024], psA[:, 512:1024], scm[:, 2:3], None, ALU.mult)
            mi_auto = v.tensor_copy(encol[:], psA[:, 1040:1041])
            rec["cready"] = ve.n + 1
            ve.inc(mi_auto)
        Vb(f_bcev)

        if "caug" in tap:
            def t_ca(g):
                ve.wait(g)
                gsp.begin(g)
                gsp.dma(g, tap["caug"][:], caug[:])
            Gb(t_ca)

        if _stop("s3c"):
            return nc, tap

        _epd = {}
        if "epd" in tap:
            _epd["sb"] = nc.alloc_sbuf_tensor_at("epd_sb", [128, 1048], F32,
                                                 offset=OB + 139264)
        def _ep_tail(v, n1p, half):
            if n1p >= 2:
                se.wait(v, rec[("atr", half, n1p - 2)])
            pbp = 2048 + 1024 * (n1p % 2)
            pvp = psA[:, pbp: pbp + 1024].rearrange("p (h c) -> p h c", h=8)
            v.tensor_tensor(psA[:, 0:512].rearrange("p (h c) -> p h c", h=8),
                            c1s.ap().rearrange("p (h c) -> p h c", h=8),
                            pvp[:, :, 0:64], ALU.add)

        def _ep_att(v, n1p, half):
            for hl in range(8):
                mi_auto = v.tensor_scalar(att_rot[:, n1p % 2, 64 * hl: 64 * (hl + 1)],
                                          psA[:, 64 * hl: 64 * (hl + 1)],
                                          ep3[:, hl: hl + 1], None, ALU.mult)
            rec[("epi", half, n1p)] = ve.n + 1
            ve.inc(mi_auto)

        def emit_p_half(half):
            def f_atr_t(t, n1, half=half):
                ve.wait(t, rec[("epi", half, n1)])
                if ("atre", half, n1 - 2) in rec:
                    se.wait(t, rec[("atre", half, n1 - 2)])
                base = 1024 + 512 * (n1 % 2)
                for jj in range(4):
                    mi_auto = t.transpose(
                        psAb[:, base + 128 * jj: base + 128 * (jj + 1)],
                        att_rot[:, n1 % 2, 128 * jj: 128 * (jj + 1)], identb[:])
                tm.inc(mi_auto)

            def f_atr_e(s, n1, half=half):
                tm.wait(s)
                base = 1024 + 512 * (n1 % 2)
                for jj in range(4):
                    mi_auto = s.copy(attn_fm[:, 4 * half + jj, 128 * n1: 128 * (n1 + 1)],
                                     psAb[:, base + 128 * jj: base + 128 * (jj + 1)])
                rec[("atr", half, n1)] = se.n + 1
                rec[("atre", half, n1)] = se.n + 1
                se.inc(mi_auto)

            for n1 in range(NCH):
                def f_pmm(t, n1=n1, half=half):
                    if n1 == 0:
                        ve.wait(t, rec["cready"])
                        slp.wait_all(t)
                        ve.wait(t)
                        se.wait(t)
                    if n1 >= 2:
                        ve.wait(t, rec[("epi", half, n1 - 2)])
                    pb = 2048 + 1024 * (n1 % 2)
                    for hl in range(8):
                        hg = 8 * half + hl
                        for mc2 in range(2):
                            t.matmul(psA[:, pb + 128 * hl: pb + 128 * hl + 66],
                                     eqT[:, hl, mc2, 128 * n1: 128 * (n1 + 1)],
                                     caug[:, mc2, hg, :], start=(mc2 == 0), stop=False,
                                     skip_group_check=True)
                        mi_auto = t.matmul(psA[:, pb + 128 * hl: pb + 128 * hl + 66],
                                 epsr[0:1, 0:128], csum[0:1, hg, :],
                                 start=False, stop=True, skip_group_check=True)
                    rec[("pmm", half, n1)] = tm.n + 1
                    tm.inc(mi_auto)
                Tb(f_pmm)

                def f_epi(v, n1=n1, half=half):
                    tm.wait(v, rec[("pmm", half, n1)])
                    if n1 == 0 and half == 0 and "epd" in tap:
                        v.tensor_copy(_epd["sb"][:, 0:1024], psA[:, 2048:3072])
                    if n1 == 0:
                        for _ in range(4):
                            v.tensor_copy(c1s[:, 0:512], vs2e.ap().rearrange(
                                "p h d -> p (h d)")[:, 0:512])
                    pb = 2048 + 1024 * (n1 % 2)
                    pv = psA[:, pb: pb + 1024].rearrange("p (h c) -> p h c", h=8)
                    v.tensor_copy(ep1[:], pv[:, :, 65:66].rearrange("p a b -> p (a b)"))
                    if n1 == 0:
                        v.tensor_scalar(jrow[:], jrow[:], 0.0, None, ALU.mult)
                    else:
                        _ep_tail(v, n1 - 1, half)
                    v.scalar_tensor_tensor(ep2[:], ep1[:], encol[:, 0:1],
                                           pv[:, :, 64:65].rearrange("p a b -> p (a b)"),
                                           ALU.mult, ALU.add)
                    if n1 > 0:
                        _ep_att(v, n1 - 1, half)
                    for hl in range(8):
                        v.tensor_scalar(c1s[:, 64 * hl: 64 * (hl + 1)],
                                        vs2e[:, 8 * half + hl, :], ep1[:, hl: hl + 1],
                                        None, ALU.mult)
                    mi_auto = v.reciprocal(ep3[:], ep2[:])
                    if n1 == 0 and half == 0 and "epd" in tap:
                        v.tensor_copy(_epd["sb"][:, 1024:1032], ep1[:])
                        v.tensor_copy(_epd["sb"][:, 1032:1040], ep2[:])
                        mi_auto = v.tensor_copy(_epd["sb"][:, 1040:1048], ep3[:])
                    ve.inc(mi_auto)
                Vb(f_epi)

                if n1 >= 1:
                    def f_atrpt(t, n1=n1, half=half):
                        f_atr_t(t, n1 - 1)
                    Tb(f_atrpt)

                    def f_atrpe(s, n1=n1, half=half):
                        f_atr_e(s, n1 - 1)
                    Sb(f_atrpe)

            def f_epit(v, half=half):
                _ep_tail(v, NCH - 1, half)
                v.tensor_scalar(jrow[:], jrow[:], 0.0, None, ALU.mult)
                _ep_att(v, NCH - 1, half)
            Vb(f_epit)

            def f_atrlt(t, half=half):
                f_atr_t(t, NCH - 1)
            Tb(f_atrlt)

            def f_atrle(s, half=half):
                f_atr_e(s, NCH - 1)
            Sb(f_atrle)

        emit_p_half(0)
        if "epd" in tap:
            def t_epd(g):
                ve.wait(g)
                gsp.begin(g)
                gsp.dma(g, tap["epd"][:], _epd["sb"][:])
            Gb(t_epd)
        emit_dd_half(1)
        emit_p_half(1)

        if "attn" in tap:
            def t_at(g):
                slp.wait_all(g)
                gsp.begin(g)
                gsp.dma(g, tap["attn"][:], attn_fm[:])
            Gb(t_at)
        if "eqt" in tap:
            def t_eqt(g):
                slp.wait_all(g)
                gsp.begin(g)
                gsp.dma(g, tap["eqt"][:], eqT[:])
            Gb(t_eqt)

        if _stop("s4p"):
            return nc, tap

        wop = woTd.rearrange("(ec p) f -> p ec f", p=128)

        def f_wol(g):
            tm.wait(g)
            slp.wait_all(g)
            ve.wait(g)
            glp.begin(g)
            glp.dma(g, woTs[:, :, :], wop)
            glp.dma(g, brf5[:], bo_row)
            rec["wol"] = glp.tok()
        Gb(f_wol)

        def f_boprep(v):
            tw(v, rec["wol"])
            v.tensor_copy(bob2[:], brf5[:])
            v.tensor_scalar(jrow[:], jrow[:], 0.0, None, ALU.mult)
            ve.inc(v.tensor_tensor(bolo[:], brf5[:], bob2[:], ALU.subtract))
        Vb(f_boprep)

        for n1 in range(NCH):
            j = n1 % 4

            def f_womm(t, n1=n1):
                tw(t, rec["wol"])
                if n1 == 0:
                    se.wait(t)
                    ve.wait(t)
                if n1 >= 2:
                    ve.wait(t, rec[("xm", n1 - 2)])
                pb = 1024 * (n1 % 2)
                for ech in range(8):
                    for fh2 in range(2):
                        t.matmul(psA[:, pb + 512 * fh2: pb + 512 * (fh2 + 1)],
                                 attn_fm[:, ech, 128 * n1: 128 * (n1 + 1)],
                                 woTs[:, ech, 512 * fh2: 512 * (fh2 + 1)],
                                 start=(ech == 0), stop=False, skip_group_check=True)
                for fh2 in range(2):
                    t.matmul(psA[:, pb + 512 * fh2: pb + 512 * (fh2 + 1)],
                             onesr[0:1, 0:128], bob2[0:1, 512 * fh2: 512 * (fh2 + 1)],
                             start=False, stop=False, skip_group_check=True)
                    mi_auto = t.matmul(psA[:, pb + 512 * fh2: pb + 512 * (fh2 + 1)],
                             onesr[0:1, 0:128], bolo[0:1, 512 * fh2: 512 * (fh2 + 1)],
                             start=False, stop=True, skip_group_check=True)
                rec[("wo", n1)] = tm.n + 1
                tm.inc(mi_auto)
            Tb(f_womm)

            def f_xm(v, n1=n1):
                tm.wait(v, rec[("wo", n1)])
                pb = 1024 * (n1 % 2)
                rec[("xm", n1)] = ve.n + 1
                ve.inc(v.tensor_tensor(xmid[:, n1, :], psA[:, pb: pb + 1024],
                                       xa[:, n1, :], ALU.add))
                if n1 == 8:
                    ve.inc(v.tensor_reduce(stat2[:, 4:5], stat[:, 8:16], AX.X, ALU.max))
            Vb(f_xm)

            if n1 % 2 == 0:
                continue
            n0 = n1 - 1
            j0, j1 = n0 % 4, n1 % 4

            if _STOP == "s5a":
                continue

            def f_l2v1(v, n0=n0, n1=n1, j0=j0, j1=j1):
                v.tensor_reduce(stat[:, j0: j0 + 1], xmid[:, n0, :], AX.X, ALU.add)
                v.tensor_reduce(stat[:, j1: j1 + 1], xmid[:, n1, :], AX.X, ALU.add)
                v.tensor_scalar(stat[:, 4 + j0: 5 + j0], stat[:, j0: j0 + 1], 1.0 / E, None, ALU.mult)
                v.tensor_scalar(stat[:, 4 + j1: 5 + j1], stat[:, j1: j1 + 1], 1.0 / E, None, ALU.mult)
                v.tensor_tensor(stat2[:, j0: j0 + 1], stat[:, 4 + j0: 5 + j0],
                                stat[:, 4 + j0: 5 + j0], ALU.mult)
                v.tensor_tensor(stat2[:, j1: j1 + 1], stat[:, 4 + j1: 5 + j1],
                                stat[:, 4 + j1: 5 + j1], ALU.mult)
                v.tensor_scalar(stat2[:, j0: j0 + 1], stat2[:, j0: j0 + 1], -1.0, LNEPS,
                                ALU.mult, ALU.add)
                ve.inc(v.tensor_scalar(stat2[:, j1: j1 + 1], stat2[:, j1: j1 + 1], -1.0, LNEPS,
                                       ALU.mult, ALU.add))
            Vb(f_l2v1)

            def f_l2s1(s, n0=n0, n1=n1, j0=j0, j1=j1):
                ve.wait(s)
                if n0 == 0:
                    tm.wait(s)
                s.activation(qtrot[:, 0, :], xmid[:, n0, :], AF.Square,
                             accum_out=scm[:, j0: j0 + 1])
                s.activation(qtrot[:, 1, :], xmid[:, n1, :], AF.Square,
                             accum_out=scm[:, j1: j1 + 1])
                s.activation(sc1[:, 0:1], scm[:, j0: j0 + 1], AF.Sqrt,
                             scale=1.0 / E, bias=stat2[:, j0: j0 + 1])
                se.inc(s.activation(sc1[:, 1:2], scm[:, j1: j1 + 1], AF.Sqrt,
                                    scale=1.0 / E, bias=stat2[:, j1: j1 + 1]))
            Sb(f_l2s1)

            def f_l2v2(v, j0=j0, j1=j1):
                se.wait(v)
                v.reciprocal(sc2[:, 0:1], sc1[:, 0:1])
                v.reciprocal(sc2[:, 1:2], sc1[:, 1:2])
                v.tensor_scalar(sc3[:, j0: j0 + 1], stat[:, 4 + j0: 5 + j0], sc2[:, 0:1],
                                -1.0, ALU.mult, ALU.mult)
                ve.inc(v.tensor_scalar(sc3[:, j1: j1 + 1], stat[:, 4 + j1: 5 + j1], sc2[:, 1:2],
                                       -1.0, ALU.mult, ALU.mult))
            Vb(f_l2v2)

            for ci, (nn, jj) in enumerate(((n0, j0), (n1, j1))):
                def f_l2s2(v, nn=nn, jj=jj, ci=ci):
                    rec[("l2", nn)] = ve.n + 1
                    ve.inc(v.tensor_scalar(ln2a[:, nn, :], xmid[:, nn, :], sc2[:, ci: ci + 1],
                                           sc3[:, jj: jj + 1], ALU.mult, ALU.add))
                Vb(f_l2s2)

            def f_l2mx(v, n0=n0, n1=n1):
                v.tensor_reduce(stat[:, 8 + n0 % 8: 9 + n0 % 8], ln2a[:, n0, :],
                                AX.X, ALU.max, apply_absolute_value=True)
                ve.inc(v.tensor_reduce(stat[:, 8 + n1 % 8: 9 + n1 % 8], ln2a[:, n1, :],
                                       AX.X, ALU.max, apply_absolute_value=True))
            Vb(f_l2mx)

        if _stop("s5a") or _stop("s5b"):
            return nc, tap

        def f_xsf(v):
            ve.wait(v)
            v.tensor_scalar(jrow[:], jrow[:], 0.0, None, ALU.mult)
            v.tensor_reduce(stat2[:, 5:6], stat[:, 8:16], AX.X, ALU.max)
            v.tensor_scalar(jrow[:], jrow[:], 0.0, None, ALU.mult)
            ve.inc(v.tensor_reduce(sc1[:, 0:1], stat2[:, 4:6], AX.X, ALU.max))
        Vb(f_xsf)

        def f_xst(t):
            ve.wait(t)
            mi_auto = t.transpose(psA[0:1, 0:128], sc1[:, 0:1], ident[:])
            tm.inc(mi_auto)
        Tb(f_xst)

        def f_xsr(v):
            tm.wait(v)
            v.tensor_reduce(srow[0:1, 24:25], psA[0:1, 0:128], AX.X, ALU.max)
            v.tensor_scalar(jrow[:], jrow[:], 0.0, None, ALU.mult)
            v.tensor_scalar(srow[0:1, 25:26], srow[0:1, 24:25], 1.0 / 127.0, LNEPS,
                            ALU.mult, ALU.max)
            v.tensor_scalar(jrow[:], jrow[:], 0.0, None, ALU.mult)
            for i in range(8):
                mi_auto = v.tensor_copy(srow[0:1, 40 + i: 41 + i], srow[0:1, 25:26])
            ve.inc(mi_auto)
        Vb(f_xsr)

        if _stop("s5c"):
            return nc, tap

        def f_arx(g):
            ve.wait(g)
            gsp.begin(g)
            gsp.dma(g, mxx_in.ap()[:, :], srow[0:1, 40:48])
            tw(g, gsp.tok())
            cc.inc(g.collective_compute(
                "AllReduce", ALU.max,
                replica_groups=[[0, 1, 2, 3, 4, 5, 6, 7]],
                ins=[mxx_in.ap().opt()], outs=[mxx_out.ap().opt()]))
            g.wait_ge(cc.h, cc.n)
            glp.begin(g)
            glp.dma(g, srow[0:1, 48:56], mxx_out.ap()[:, :])
            rec["xs_done"] = glp.tok()
        Gb(f_arx)

        for tname in ("xmid", "ln2", "mxx"):
            if tname in tap:
                def t_t(g, tname=tname):
                    if tname == "mxx":
                        cc.wait(g)
                        gsp.begin(g)
                        gsp.dma(g, tap["mxx"][:, :], mxx_out.ap()[:, :])
                    else:
                        src2 = xmid if tname == "xmid" else ln2a
                        se.wait(g)
                        ve.wait(g)
                        gsp.begin(g)
                        gsp.dma(g, tap[tname][:], src2[:])
                Gb(t_t)

        if _stop("s5"):
            return nc, tap

        w8p = w8Td.rearrange("(ec p) f -> p ec f", p=128)

        def f_wfl(g):
            tm.wait(g)
            se.wait(g)
            glp.begin(g)
            glp.dma(g, w8s[:, :, :], w8p)
            glp.dma(g, wsr[:], ws_row)
            glp.dma(g, fbrow[:], fb_row)
            rec["wfl"] = glp.tok()
        Gb(f_wfl)

        def f_xsc(v):
            tw(v, rec["xs_done"])
            mi_auto = v.reciprocal(srow[0:1, 56:57], srow[0:1, 48:49])
            ve.inc(mi_auto)
        Vb(f_xsc)

        def f_xbc(t):
            ve.wait(t)
            tw(t, rec["wfl"])
            t.matmul(psA[:, 0:1], onesf[0:1, :], srow[0:1, 48:49], start=True, stop=True)
            t.matmul(psA[:, 512:1024], onesf[0:1, :], wsr[0:1, 0:512], start=True, stop=True)
            mi_auto = t.matmul(psA[:, 1024:1536], onesf[0:1, :], wsr[0:1, 512:1024], start=True, stop=True)
            tm.inc(mi_auto)
        Tb(f_xbc)

        def f_xbev(v):
            tm.wait(v)
            v.tensor_copy(sc1[:, 0:1], psA[:, 0:1])
            v.tensor_copy(wsbc[:, 0:512], psA[:, 512:1024])
            v.tensor_copy(wsbc[:, 512:1024], psA[:, 1024:1536])
            v.tensor_scalar(fbp[0:1, :], fbrow[:], srow[0:1, 56:57], None, ALU.mult)
            ve.inc(v.reciprocal(sc2[:, 0:1], sc1[:, 0:1]))
        Vb(f_xbev)

        for ech in range(8):
            def f_wfb(v, ech=ech):
                if ech == 0:
                    tm.wait(v)
                    se.wait(v)
                ve.wait(v)
                v.tensor_copy(qtrot[:, 1, :], w8s[:, ech, :])
                ve.inc(v.tensor_tensor(wfbf[:, ech, :], qtrot[:, 1, :], wsbc[:], ALU.mult))
            Vb(f_wfb)

        def f_idl(g):
            ve.wait(g)
            glp.begin(g)
            glp.dma(g, identb2[:], identb_c.ap())
            rec["idl"] = glp.tok()
        Gb(f_idl)

        for n1 in range(NCH):
            def f_qza(v, n1=n1):
                ve.wait(v)
                if n1 == 0:
                    slp.wait_all(v)
                if n1 >= 2:
                    ve.wait(v, rec[("qtr", n1 - 2)])
                mi_auto = v.tensor_scalar(psA[:, 3072:4096], ln2a[:, n1, :],
                                          sc2[:, 0:1], None, ALU.mult)
                ve.inc(mi_auto)
            Vb(f_qza)

            def f_qzs(s, n1=n1):
                ve.wait(s)
                se.inc(s.activation(qsg[:], psA[:, 3072:4096], AF.Sign))
            Sb(f_qzs)

            def f_qz(v, n1=n1):
                se.wait(v)
                v.tensor_tensor(psA[:, 3072:4096], psA[:, 3072:4096], qsg[:], ALU.mult)
                v.tensor_scalar(qw[:], psA[:, 3072:4096], 128.0, None, ALU.add)
                v.scalar_tensor_tensor(i8rot[:, n1 % 2, :], qw[:], -128.0, qsg[:],
                                       ALU.add, ALU.mult)
                rec[("qz", n1)] = ve.n + 1
                ve.inc(v.tensor_copy(qtrot[:, n1 % 2, :], i8rot[:, n1 % 2, :]))
            Vb(f_qz)

            def f_qtr_t(t, n1=n1):
                ve.wait(t, rec[("qz", n1)])
                if n1 == 0:
                    tm.wait(t)
                if n1 >= 2:
                    ve.wait(t, rec[("qtr", n1 - 2)])
                base = 4096 + 1024 * (n1 % 2)
                if n1 == 0:
                    tw(t, rec["idl"])
                for ech in range(8):
                    mi_auto = t.transpose(
                        psAb[:, base + 128 * ech: base + 128 * (ech + 1)],
                        qtrot[:, n1 % 2, 128 * ech: 128 * (ech + 1)], identb2[:])
                tm.inc(mi_auto)
            Tb(f_qtr_t)

            def f_qtr_e(v, n1=n1):
                tm.wait(v)
                base = 4096 + 1024 * (n1 % 2)
                for ech in range(8):
                    mi_auto = v.tensor_copy(
                        attn_fm[:, ech, 128 * n1: 128 * (n1 + 1)],
                        psAb[:, base + 128 * ech: base + 128 * (ech + 1)])
                rec[("qtr", n1)] = ve.n + 1
                ve.inc(mi_auto)
            Vb(f_qtr_e)

            def f_fmm(t, n1=n1):
                ve.wait(t, rec[("qtr", n1)])
                if n1 == 0:
                    ve.wait(t)
                if n1 >= 2:
                    ve.wait(t, rec[("fev", n1 - 2)])
                pb = 1024 * (n1 % 2)
                for ech in range(8):
                    for fh2 in range(2):
                        t.matmul(psA[:, pb + 512 * fh2: pb + 512 * (fh2 + 1)],
                                 attn_fm[:, ech, 128 * n1: 128 * (n1 + 1)],
                                 wfbf[:, ech, 512 * fh2: 512 * (fh2 + 1)],
                                 start=(ech == 0), stop=False, skip_group_check=True)
                for fh2 in range(2):
                    mi_auto = t.matmul(psA[:, pb + 512 * fh2: pb + 512 * (fh2 + 1)],
                             onesr[0:1, 0:128], fbp[0:1, 512 * fh2: 512 * (fh2 + 1)],
                             start=False, stop=True, skip_group_check=True)
                rec[("fmm", n1)] = tm.n + 1
                tm.inc(mi_auto)
            Tb(f_fmm)

            def f_fev(v, n1=n1):
                tm.wait(v, rec[("fmm", n1)])
                pb = 1024 * (n1 % 2)
                v.scalar_tensor_tensor(outa[:, n1, 0:512], psA[:, pb: pb + 512],
                                       sc1[:, 0:1], xmid[:, n1, 0:512], ALU.mult, ALU.add)
                rec[("fev", n1)] = ve.n + 1
                ve.inc(v.scalar_tensor_tensor(outa[:, n1, 512:1024],
                                              psA[:, pb + 512: pb + 1024],
                                              sc1[:, 0:1], xmid[:, n1, 512:1024],
                                              ALU.mult, ALU.add))
            Vb(f_fev)

            def f_yst(g, n1=n1):
                ve.wait(g, rec[("fev", n1)])
                gsp.begin(g)
                gsp.dma(g, y_out.rearrange("(c p) f -> c p f", p=128)[n1],
                        outa[:, n1, :])
            Gb(f_yst)

        def f_fin(g):
            gsp.wait_all(g)
        Gb(f_fin)

    return nc, tap



def _shard_inputs(inputs):
    x = np.ascontiguousarray(np.asarray(inputs["L0_attn_x"], np.float32))
    Wq = np.asarray(inputs["L0_attn_Wq"], np.float32)
    Wk = np.asarray(inputs["L0_attn_Wk"], np.float32)
    Wv = np.asarray(inputs["L0_attn_Wv"], np.float32)
    Wo = np.asarray(inputs["L0_attn_Wo"], np.float32)
    bqv = np.asarray(inputs["L0_attn_bq"], np.float32)
    bkv = np.asarray(inputs["L0_attn_bk"], np.float32)
    bvv = np.asarray(inputs["L0_attn_bv"], np.float32)
    bov = np.asarray(inputs["L0_attn_bo"], np.float32)
    proj = np.asarray(inputs["L0_attn_proj"], np.float32)
    w8 = np.asarray(inputs["L0_ffn_weight_int8"], np.int8)
    ws = np.asarray(inputs["L0_ffn_weight_scale"], np.float32)
    fb = np.asarray(inputs["L0_ffn_bias"], np.float32)

    wqTh = np.ascontiguousarray(Wq.T)
    woTh = np.ascontiguousarray(Wo.T)
    pT = np.ascontiguousarray(proj.T)
    projTh = np.zeros((128, 512), np.float32)
    projTh[0:64, 0:256] = pT
    projTh[64:128, 256:512] = pT
    w8Th = np.ascontiguousarray(w8.T)
    bo_rh = np.ascontiguousarray(bov.reshape(1, E))
    ws_rh = np.ascontiguousarray(ws.reshape(1, E))
    fb_rh = np.ascontiguousarray(fb.reshape(1, E))

    maps = []
    for c in range(8):
        b, hf = c // 2, c % 2
        hsl = slice(512 * hf, 512 * (hf + 1))
        maps.append({
            "x_own": np.ascontiguousarray(x[b, 2048 * hf: 2048 * (hf + 1)]),
            "x_par": np.ascontiguousarray(x[b, 2048 * (1 - hf): 2048 * (2 - hf)]),
            "wqT": wqTh,
            "wkT": np.ascontiguousarray(Wk[hsl].T),
            "wvT": np.ascontiguousarray(Wv[hsl].T),
            "woT": woTh,
            "projT": projTh,
            "bq_row": np.ascontiguousarray(bqv.reshape(1, E)),
            "bk_row": np.ascontiguousarray(bkv[hsl].reshape(1, 512)),
            "bv_row": np.ascontiguousarray(bvv[hsl].reshape(1, 512)),
            "bo_row": bo_rh,
            "w8T": w8Th,
            "ws_row": ws_rh,
            "fb_row": fb_rh,
        })
    return maps


def kernel(**inputs):
    maps = _shard_inputs(inputs)
    nc, _ = build_nc(taps=_TAPS)
    res = run_bass_kernel_spmd(nc, maps, core_ids=list(range(8)))
    out = np.empty((B, NSEQ, E), np.float32)
    for c in range(8):
        b, hf = c // 2, c % 2
        out[b, 2048 * hf: 2048 * (hf + 1)] = res.results[c]["y"]
    if _TAPS:
        kernel.last = res
    return out


# version 05
# speedup vs baseline: 1.0265x; 1.0265x over previous
import numpy as np
import ml_dtypes
from contextlib import ExitStack
import concourse.bass as bass
import concourse.mybir as mybir
from concourse.bass_utils import run_bass_kernel_spmd

F32 = mybir.dt.float32
BF16 = mybir.dt.bfloat16
I8 = mybir.dt.int8
AF = mybir.ActivationFunctionType
ALU = mybir.AluOpType
AX = mybir.AxisListType

E, H, D, M = 1024, 16, 64, 256
B, NSEQ, NT = 4, 4096, 2048
NCH, KCH = 16, 32
S2C = float(D ** (-0.25))
KEPS = 1e-4
LNK = float(np.log(KEPS))
LNEPS = 1e-5
BF = ml_dtypes.bfloat16

_TAPS = ()
_STOP = None


def _bfc(a):
    return np.ascontiguousarray(np.asarray(a, np.float32)).astype(BF)


class Sem:
    __slots__ = ("h", "n")

    def __init__(self, ctx, nc, name):
        self.h = ctx.enter_context(nc.semaphore(name))
        self.n = 0

    def inc(self, instr, k=1):
        instr.then_inc(self.h, k)
        self.n += k
        return instr

    def wait(self, eng, n=None):
        n = self.n if n is None else n
        assert n <= self.n, ("wait beyond emitted", n, self.n)
        if n > 0:
            eng.wait_ge(self.h, n)


OB = 16512


class DPool:

    def __init__(self, ctx, nc, name, k=4):
        self.sems = [Sem(ctx, nc, name + str(i)) for i in range(k)]
        self.i = 0
        self.cur = None

    def begin(self, eng):
        s = self.sems[self.i]
        self.i = (self.i + 1) % len(self.sems)
        if s.n:
            eng.wait_ge(s.h, s.n)
        self.cur = s
        return s

    def dma(self, eng, out, in_, **kw):
        s = self.cur
        eng.dma_start(out=out, in_=in_, **kw).then_inc(s.h, 16)
        s.n += 16

    def tok(self):
        return (self.cur, self.cur.n)

    def wait_all(self, eng):
        for s in self.sems:
            if s.n:
                eng.wait_ge(s.h, s.n)


def tw(eng, t):
    s, n = t
    assert n <= s.n
    eng.wait_ge(s.h, n)


class Arena:
    def __init__(self, nc, base, end):
        base, end = OB + base, OB + end
        self.nc, self.base, self.end, self.off = nc, base, end, base

    def at(self, name, shape, dtype):
        per = int(np.prod(shape[1:])) * mybir.dt.size(dtype)
        off = (self.off + 31) // 32 * 32
        h = self.nc.alloc_sbuf_tensor_at(name, shape, dtype, offset=off)
        self.off = off + per
        assert self.off <= self.end, (name, self.off, self.end)
        return h


def build_nc(taps=()):
    nc = bass.Bass("TRN2", target_bir_lowering=False, detect_race_conditions=False)
    dt = nc.dram_tensor

    x_own = dt("x_own", [NT, E], F32, kind="ExternalInput").ap()
    x_par = dt("x_par", [NT, E], F32, kind="ExternalInput").ap()
    wqT = dt("wqT", [E, E], F32, kind="ExternalInput").ap()
    wkT = dt("wkT", [E, 512], F32, kind="ExternalInput").ap()
    wvT = dt("wvT", [E, 512], F32, kind="ExternalInput").ap()
    woTd = dt("woT", [E, E], F32, kind="ExternalInput").ap()
    projTd = dt("projT", [128, 2 * M], F32, kind="ExternalInput").ap()
    bq_row = dt("bq_row", [1, E], F32, kind="ExternalInput").ap()
    bk_row = dt("bk_row", [1, 512], F32, kind="ExternalInput").ap()
    bv_row = dt("bv_row", [1, 512], F32, kind="ExternalInput").ap()
    bo_row = dt("bo_row", [1, E], F32, kind="ExternalInput").ap()
    w8Td = dt("w8T", [E, E], I8, kind="ExternalInput").ap()
    ws_row = dt("ws_row", [1, E], F32, kind="ExternalInput").ap()
    fb_row = dt("fb_row", [1, E], F32, kind="ExternalInput").ap()
    y_out = dt("y", [NT, E], F32, kind="ExternalOutput").ap()

    d_qp = dt("d_qp", [8, 128, NT], BF16).ap()
    d_vtok = dt("d_vtok", [KCH, 128, 512], BF16).ap()
    d_dqT = dt("d_dqT", [NCH, 128, 16], F32).ap()
    ag_in = dt("ag_in", [8, 66, M], F32)
    ag_out = dt("ag_out", [16, 66, M], F32)
    mxk_in = dt("mxk_in", [1, 8], F32)
    mxk_out = dt("mxk_out", [1, 8], F32)
    mxx_in = dt("mxx_in", [1, 8], F32)
    mxx_out = dt("mxx_out", [1, 8], F32)

    tap = {}
    tapinfo = {
        "ln1T": ([8, 128, NSEQ], BF16), "qp": ([8, 128, NT], BF16),
        "vtok": ([KCH, 128, 512], BF16), "kp": ([4, 128, NSEQ], BF16),
        "edk": ([128, KCH, 8], F32), "dqT": ([NCH, 128, 16], F32),
        "gx": ([8, 66, M], F32), "mxk": ([1, 8], F32), "gf": ([16, 66, M], F32),
        "caug": ([128, 2, 16, 66], BF16), "attn": ([128, 8, NT], BF16),
        "xmid": ([128, NCH, E], F32), "ln2": ([128, NCH, E], F32),
        "mxx": ([1, 8], F32), "eqt": ([128, 8, 2, NT], BF16),
        "epd": ([128, 1048], F32),
    }
    for t in taps:
        shp, dtp = tapinfo[t]
        tap[t] = dt("dbg_" + t, shp, dtp, kind="ExternalOutput").ap()

    ident_c = nc.inline_tensor(np.eye(128, dtype=np.float32), "ident_c")
    identb_c = nc.inline_tensor(np.eye(128, dtype=ml_dtypes.bfloat16), "identb_c")
    sq_np = np.zeros((128, 8, 16), np.float32)
    for fc in range(8):
        sq_np[0:64, fc, 2 * fc] = 1.0
        sq_np[64:128, fc, 2 * fc + 1] = 1.0
    selq_c = nc.inline_tensor(_bfc(sq_np.reshape(128, 128)), "selq_c")
    sk_np = np.zeros((128, 4, 8), np.float32)
    for fc in range(4):
        sk_np[0:64, fc, 2 * fc] = 1.0
        sk_np[64:128, fc, 2 * fc + 1] = 1.0
    selk_c = nc.inline_tensor(_bfc(sk_np.reshape(128, 32)), "selk_c")
    onesr_c = nc.inline_tensor(_bfc(np.ones((1, 128))), "onesr_c")
    ones4_c = nc.inline_tensor(_bfc(np.ones((1, 512))), "ones4_c")
    epsr_c = nc.inline_tensor(_bfc(np.full((1, 128), KEPS)), "epsr_c")
    onesc_c = nc.inline_tensor(_bfc(np.ones((128, 1))), "onesc_c")
    onesf_c = nc.inline_tensor(np.ones((1, 128), np.float32), "onesf_c")
    zbf_c = nc.inline_tensor(_bfc(np.zeros((1, 96))), "zbf_c")

    with ExitStack() as ctx:
        en = ctx.enter_context
        assert nc.SBUF_PARTITION_SIZE_BYTES >= OB + 212832, nc.SBUF_PARTITION_SIZE_BYTES

        R0 = Arena(nc, 0, 65536)
        xa = R0.at("xa", [128, NCH, E], F32)
        ln2a = Arena(nc, 0, 65536).at("ln2a", [128, NCH, E], F32)
        outa = Arena(nc, 0, 65536).at("outa", [128, NCH, E], F32)

        ln1T = Arena(nc, 65536, 131072).at("ln1T", [128, 8, NSEQ], BF16)
        eqT = Arena(nc, 65536, 131072).at("eqT", [128, 8, 2, NT], BF16)
        xmid = Arena(nc, 65536, 131072).at("xmid", [128, NCH, E], F32)

        r2s1 = Arena(nc, 131072, 163840)
        qrot = r2s1.at("qrot", [128, 4, 512], BF16)
        sqrot = r2s1.at("sqrot", [128, 4, 512], BF16)
        dgrot = r2s1.at("dgrot", [128, 4, 512], F32)
        vrot = r2s1.at("vrot", [128, 2, 512], BF16)
        edk_sb = r2s1.at("edk", [128, KCH, 8], F32)
        vss = r2s1.at("vss", [1, 512], F32)
        zrow = r2s1.at("zrow", [1, 256], F32)
        gx = Arena(nc, 131072, 139264).at("gx", [128, 2048], F32)
        r2s2 = Arena(nc, 139264, 147456)
        vtrot = r2s2.at("vtrot", [128, 2, 512], BF16)
        vvrot = r2s2.at("vvrot", [128, 2, 8, 66], BF16)
        kmaxt = r2s2.at("kmaxt", [128, 512], BF16)
        ekrot = Arena(nc, 155648, 163840).at("ekrot", [128, 2, 8, M], BF16)
        gf = Arena(nc, 131072, 147456).at("gf", [128, 4096], F32)
        r2s4a = Arena(nc, 147456, 155648)
        vsr = r2s4a.at("vsr", [1, 1024], F32)
        dq_all = r2s4a.at("dq_all", [128, NCH, 16], F32)
        ep1 = r2s4a.at("ep1", [128, 8], F32)
        ep2 = r2s4a.at("ep2", [128, 8], F32)
        ep3 = r2s4a.at("ep3", [128, 8], F32)
        c1s = r2s4a.at("c1s", [128, 512], F32)
        eqrot = Arena(nc, 155648, 163840).at("eqrot", [128, 1, NCH, M], BF16)
        r2pc = Arena(nc, 131072, 147456)
        vs2e = r2pc.at("vs2e", [128, 16, D], BF16)
        csum = r2pc.at("csum", [1, 16, 66], BF16)
        kse = r2pc.at("kse", [128, 16], F32)
        encol = r2pc.at("encol", [128, 1], F32)
        att_rot = r2pc.at("att_rot", [128, 2, 512], BF16)
        r2s5 = Arena(nc, 131072, 155648)
        woTs = r2s5.at("woTs", [128, 8, E], BF16)
        bob2 = r2s5.at("bob2", [1, E], BF16)
        brf5 = r2s5.at("brf5", [1, E], F32)
        bolo = r2s5.at("bolo", [1, E], BF16)
        r2s6 = Arena(nc, 131072, 163840)
        w8s = r2s6.at("w8s", [128, 8, E], I8)
        r2s6b = Arena(nc, 131072, 139264)
        qw = r2s6b.at("qw", [128, E], BF16)
        qsg = r2s6b.at("qsg", [128, E], BF16)
        wfbf = r2s6.at("wfbf", [128, 8, E], BF16)
        i8rot = r2s6.at("i8rot", [128, 2, E], I8)
        qtrot = r2s6.at("qtrot", [128, 2, E], BF16)
        fbp = r2s6.at("fbp", [1, E], BF16)

        kp = Arena(nc, 163840, 196608).at("kp", [128, 4, NSEQ], BF16)
        attn_fm = Arena(nc, 163840, 196608).at("attn_fm", [128, 8, NT], BF16)

        r4s0 = Arena(nc, 196608, 209024)
        xprot = r4s0.at("xprot", [128, 2, E], F32)
        ltrot = r4s0.at("ltrot", [128, 2, E], BF16)
        wq0 = Arena(nc, 196608, 204800).at("wq0", [128, 8, 512], BF16)
        wk = Arena(nc, 196608, 204800).at("wk", [128, 8, 512], BF16)
        wv = Arena(nc, 196608, 204800).at("wv", [128, 8, 512], BF16)
        qprot = Arena(nc, 196608, 204800).at("qprot", [128, 2, NT], BF16)
        caug = Arena(nc, 204800, 209024).at("caug", [128, 2, 16, 66], BF16)
        r4bq = Arena(nc, 204800, 209024)
        bqb = r4bq.at("bqb", [1, E], BF16)
        bqlo = r4bq.at("bqlo", [1, E], BF16)
        r4bk = Arena(nc, 204800, 209024)
        bkb = r4bk.at("bkb", [1, 512], BF16)
        bklo = r4bk.at("bklo", [1, 512], BF16)
        bvb2 = r4bk.at("bvb2", [1, 512], BF16)
        bvlo = r4bk.at("bvlo", [1, 512], BF16)
        brf = Arena(nc, 147456, 155648).at("brf", [1, E], F32)
        ones4 = Arena(nc, 153600, 155648).at("ones4", [1, 512], BF16)
        r4s6 = Arena(nc, 196608, 204800)
        wsbc = r4s6.at("wsbc", [128, E], F32)
        wsr = r4s6.at("wsr", [1, E], F32)
        fbrow = Arena(nc, 204800, 209024).at("fbrow", [1, E], F32)

        pm = Arena(nc, 209024, 212832)
        ident = pm.at("ident", [128, 128], F32)
        identb = Arena(nc, 154752, 155648).at("identb", [128, 128], BF16)
        identb2 = Arena(nc, 135168, 139264).at("identb2", [128, 128], BF16)
        selq = pm.at("selq", [128, 128], BF16)
        selk = pm.at("selk", [128, 32], BF16)
        onesr = pm.at("onesr", [1, 128], BF16)
        epsr = pm.at("epsr", [1, 128], BF16)
        onesc = pm.at("onesc", [128, 1], BF16)
        onesf = pm.at("onesf", [1, 128], F32)
        zbf = pm.at("zbf", [1, 96], BF16)
        projT = pm.at("projT", [128, 2 * M], BF16)
        stat = pm.at("stat", [128, 16], F32)
        scm = pm.at("scm", [128, 4], F32)
        stat2 = pm.at("stat2", [128, 8], F32)
        sc1 = pm.at("sc1", [128, 2], F32)
        sc2 = pm.at("sc2", [128, 2], F32)
        sc3 = pm.at("sc3", [128, 16], F32)
        srow = pm.at("srow", [1, 64], F32)
        jrow = pm.at("jrow", [1, 8], F32)

        psA = en(nc.psum_tensor("psA", [128, 4096], F32))
        psAb = psA.bitcast(BF16)

        glp = DPool(ctx, nc, "gl", 4)
        slp = DPool(ctx, nc, "sl", 4)
        gsp = DPool(ctx, nc, "gs", 4)
        tm = Sem(ctx, nc, "tm")
        ve = Sem(ctx, nc, "ve")
        se = Sem(ctx, nc, "se")
        cc = Sem(ctx, nc, "cc")
        blk = en(nc.Block())
        rec = {}

        Gb, Tb, Vb, Sb, Yb = blk.gpsimd, blk.tensor, blk.vector, blk.scalar, blk.sync

        def _stop(label):
            if _STOP != label:
                return False

            def f_stop(g):
                gsp.begin(g)
                yv = y_out.rearrange("(c p) f -> c p f", p=128)
                for n1 in range(16):
                    gsp.dma(g, yv[n1], outa[:, n1, :])
            Gb(f_stop)

            def f_sfin(g):
                gsp.wait_all(g)
            Gb(f_sfin)
            return True

        def c_consts(g):
            glp.begin(g)
            for dst, src in ((ident, ident_c), (selq, selq_c), (selk, selk_c),
                             (onesr, onesr_c), (epsr, epsr_c), (onesc, onesc_c),
                             (onesf, onesf_c)):
                glp.dma(g, dst[:], src.ap())
            glp.dma(g, projT[:], projTd)
            glp.dma(g, zbf[:], zbf_c.ap())
            glp.dma(g, identb[:], identb_c.ap())
            glp.dma(g, ones4[:], ones4_c.ap())
            rec["consts"] = glp.tok()
        Gb(c_consts)

        xv = x_own.rearrange("(c p) f -> c p f", p=128)
        pv = x_par.rearrange("(c p) f -> c p f", p=128)

        for c in range(KCH):
            def f_load(g, c=c):
                if c >= NCH and c - NCH >= 2:
                    ve.wait(g, rec[("apl", c - 2)])
                glp.begin(g)
                if c < NCH:
                    glp.dma(g, xa[:, c, :], xv[c])
                else:
                    glp.dma(g, xprot[:, c % 2, :], pv[c - NCH])
                rec[("xl", c)] = glp.tok()
            Gb(f_load)

            if c % 2 == 0:
                continue
            c0, c1 = c - 1, c
            j0, j1 = c0 % 4, c1 % 4
            src0 = xa[:, c0, :] if c0 < NCH else xprot[:, c0 % 2, :]
            src1 = xa[:, c1, :] if c1 < NCH else xprot[:, c1 % 2, :]

            def f_v1(v, src0=src0, src1=src1, j0=j0, j1=j1, c0=c0, c1=c1):
                if c0 == 0:
                    v.memset(jrow[:], 0.0)
                tw(v, rec[("xl", c0)])
                tw(v, rec[("xl", c1)])
                v.tensor_reduce(stat[:, j0: j0 + 1], src0, AX.X, ALU.add)
                v.tensor_reduce(stat[:, j1: j1 + 1], src1, AX.X, ALU.add)
                v.tensor_scalar(stat[:, 4 + j0: 5 + j0], stat[:, j0: j0 + 1], 1.0 / E, None, ALU.mult)
                v.tensor_scalar(stat[:, 4 + j1: 5 + j1], stat[:, j1: j1 + 1], 1.0 / E, None, ALU.mult)
                v.tensor_tensor(stat2[:, j0: j0 + 1], stat[:, 4 + j0: 5 + j0], stat[:, 4 + j0: 5 + j0], ALU.mult)
                v.tensor_tensor(stat2[:, j1: j1 + 1], stat[:, 4 + j1: 5 + j1], stat[:, 4 + j1: 5 + j1], ALU.mult)
                v.tensor_scalar(stat2[:, j0: j0 + 1], stat2[:, j0: j0 + 1], -1.0, LNEPS,
                                ALU.mult, ALU.add)
                ve.inc(v.tensor_scalar(stat2[:, j1: j1 + 1], stat2[:, j1: j1 + 1], -1.0, LNEPS,
                                       ALU.mult, ALU.add))
            Vb(f_v1)

            def f_s1(s, src0=src0, src1=src1, j0=j0, j1=j1, c0=c0, c1=c1):
                tw(s, rec[("xl", c0)])
                tw(s, rec[("xl", c1)])
                if c0 >= 2:
                    ve.wait(s, rec[("ltc", c1 - 2)])
                s.activation(ltrot[:, 0, :], src0, AF.Square, accum_out=scm[:, j0: j0 + 1])
                s.activation(ltrot[:, 1, :], src1, AF.Square, accum_out=scm[:, j1: j1 + 1])
                ve.wait(s)
                s.activation(sc1[:, 0:1], scm[:, j0: j0 + 1], AF.Sqrt,
                             scale=1.0 / E, bias=stat2[:, j0: j0 + 1])
                se.inc(s.activation(sc1[:, 1:2], scm[:, j1: j1 + 1], AF.Sqrt,
                                    scale=1.0 / E, bias=stat2[:, j1: j1 + 1]))
            Sb(f_s1)

            def f_v2(v, j0=j0, j1=j1):
                se.wait(v)
                v.reciprocal(sc2[:, 0:1], sc1[:, 0:1])
                v.reciprocal(sc2[:, 1:2], sc1[:, 1:2])
                v.tensor_scalar(sc3[:, j0: j0 + 1], stat[:, 4 + j0: 5 + j0], sc2[:, 0:1],
                                -1.0, ALU.mult, ALU.mult)
                ve.inc(v.tensor_scalar(sc3[:, j1: j1 + 1], stat[:, 4 + j1: 5 + j1], sc2[:, 1:2],
                                       -1.0, ALU.mult, ALU.mult))
            Vb(f_v2)

            for ci, (cn, jj, srcc) in enumerate(((c0, j0, src0), (c1, j1, src1))):
                def f_s2(v, srcc=srcc, jj=jj, cn=cn, ci=ci):
                    rec[("apl", cn)] = ve.n + 1
                    ve.inc(v.tensor_scalar(ltrot[:, ci, :], srcc, sc2[:, ci: ci + 1],
                                           sc3[:, jj: jj + 1], ALU.mult, ALU.add))
                Vb(f_s2)

                def f_yt(t, cn=cn, ci=ci):
                    ve.wait(t, rec[("apl", cn)])
                    if cn >= 2:
                        ve.wait(t, rec[("ltc", cn - 2)])
                    base = 4096 + 1024 * (cn % 2)
                    for ech in range(8):
                        mi_auto = t.transpose(
                            psAb[:, base + 128 * ech: base + 128 * (ech + 1)],
                            ltrot[:, ci, 128 * ech: 128 * (ech + 1)], identb[:])
                    tm.inc(mi_auto)
                Tb(f_yt)

                def f_ye(v, cn=cn):
                    tm.wait(v)
                    base = 4096 + 1024 * (cn % 2)
                    for ech in range(8):
                        mi_auto = v.tensor_copy(
                            ln1T[:, ech, 128 * cn: 128 * (cn + 1)],
                            psAb[:, base + 128 * ech: base + 128 * (ech + 1)])
                    rec[("ltc", cn)] = ve.n + 1
                    ve.inc(mi_auto)
                Vb(f_ye)

        if "ln1T" in tap:
            def t_ln1T(g):
                slp.wait_all(g)
                gsp.begin(g)
                for ech in range(8):
                    gsp.dma(g, tap["ln1T"][ech], ln1T[:, ech, :])
            Gb(t_ln1T)

        if _stop("s0"):
            return nc, tap

        wqp = wqT.rearrange("(ec p) f -> p ec f", p=128)
        wkp = wkT.rearrange("(ec p) f -> p ec f", p=128)
        wvp = wvT.rearrange("(ec p) f -> p ec f", p=128)

        def f_wq0(g):
            ve.wait(g)
            ve.wait(g, rec[("ltc", KCH - 1)])
            glp.begin(g)
            glp.dma(g, wq0[:, :, :], wqp[:, :, 0:512])
            glp.dma(g, brf[:], bq_row)
            rec["wq0"] = glp.tok()
        Gb(f_wq0)

        def f_bqprep(v):
            tw(v, rec["wq0"])
            v.tensor_copy(bqb[:], brf[:])
            v.tensor_scalar(jrow[:], jrow[:], 0.0, None, ALU.mult)
            ve.inc(v.tensor_tensor(bqlo[:], brf[:], bqb[:], ALU.subtract))
        Vb(f_bqprep)

        if _stop("s1w"):
            return nc, tap

        for fch in range(8):
            fh, fi = fch // 4, fch % 4
            if fch == 4:
                def f_wq1(g):
                    tm.wait(g)
                    glp.begin(g)
                    glp.dma(g, wq0[:, :, :], wqp[:, :, 512:1024])
                    rec["wq1"] = glp.tok()
                Gb(f_wq1)

            for tih in range(2):
                def f_qmm(t, fch=fch, fh=fh, fi=fi, tih=tih):
                    tw(t, rec["wq1"] if fh else rec["wq0"])
                    tw(t, rec["consts"])
                    if fch == 0 and tih == 0:
                        ve.wait(t)
                    if fch > 0:
                        se.wait(t, rec[("qev", fch - 1, tih)])
                        ve.wait(t, rec[("qcv", fch - 1, tih)])
                    for ech in range(8):
                        for ti in (2 * tih, 2 * tih + 1):
                            t.matmul(psA[:, 512 * ti: 512 * (ti + 1)],
                                     wq0[:, ech, 128 * fi: 128 * (fi + 1)],
                                     ln1T[:, ech, 512 * ti: 512 * (ti + 1)],
                                     start=(ech == 0), stop=False)
                    for ti in (2 * tih, 2 * tih + 1):
                        t.matmul(psA[:, 512 * ti: 512 * (ti + 1)],
                                 bqb[0:1, 128 * fch: 128 * (fch + 1)], ones4[0:1, :],
                                 start=False, stop=False)
                        mi_auto = t.matmul(psA[:, 512 * ti: 512 * (ti + 1)],
                                 bqlo[0:1, 128 * fch: 128 * (fch + 1)], ones4[0:1, :],
                                 start=False, stop=True)
                    rec[("qmm", fch, tih)] = tm.n + 1
                    tm.inc(mi_auto)
                Tb(f_qmm)

                def f_qcv(v, fch=fch, tih=tih):
                    tm.wait(v, rec[("qmm", fch, tih)])
                    if fch > 0:
                        tw(v, rec[("qst", fch - 1)])
                    for ti in (2 * tih, 2 * tih + 1):
                        mi_auto = v.tensor_scalar(qrot[:, ti, :],
                                                  psA[:, 512 * ti: 512 * (ti + 1)],
                                                  S2C, None, ALU.mult)
                    rec[("qcv", fch, tih)] = ve.n + 1
                    ve.inc(mi_auto)
                Vb(f_qcv)

                def f_qev(s, fch=fch, tih=tih):
                    tm.wait(s, rec[("qmm", fch, tih)])
                    for ti in (2 * tih, 2 * tih + 1):
                        mi_auto = s.activation(sqrot[:, ti, :],
                                               psA[:, 512 * ti: 512 * (ti + 1)],
                                               AF.Square, scale=S2C)
                    rec[("qev", fch, tih)] = se.n + 1
                    se.inc(mi_auto)
                Sb(f_qev)

            def f_qst(g, fch=fch):
                ve.wait(g, rec[("qcv", fch, 1)])
                gsp.begin(g)
                for ti in range(4):
                    gsp.dma(g, d_qp[fch][:, 512 * ti: 512 * (ti + 1)], qrot[:, ti, :])
                rec[("qst", fch)] = gsp.tok()
            Gb(f_qst)

            def f_qdg(t, fch=fch):
                se.wait(t)
                for ti in range(4):
                    mi_auto = t.matmul(psA[0:16, 2048 + 512 * ti: 2048 + 512 * (ti + 1)],
                             selq[:, 16 * fch: 16 * (fch + 1)], sqrot[:, ti, :],
                             start=(fch == 0), stop=(fch == 7), skip_group_check=True)
                tm.inc(mi_auto)
            Tb(f_qdg)

        if _stop("s1q"):
            return nc, tap
        def f_dqev(s):
            tm.wait(s)
            for ti in range(4):
                mi_auto = s.copy(dgrot[0:16, ti, :], psA[0:16, 2048 + 512 * ti: 2048 + 512 * (ti + 1)])
            se.inc(mi_auto)
        Sb(f_dqev)

        def f_dqtr(t):
            se.wait(t)
            for ti in range(4):
                for sub in range(4):
                    mi_auto = t.transpose(psA[:, 512 * ti + 16 * sub: 512 * ti + 16 * (sub + 1)],
                                dgrot[0:16, ti, 128 * sub: 128 * (sub + 1)], ident[0:16, 0:16])
            tm.inc(mi_auto)
        Tb(f_dqtr)

        def f_dqev2(s):
            tm.wait(s)
            for ti in range(4):
                for sub in range(4):
                    mi_auto = s.activation(dgrot[:, ti, 16 * sub: 16 * (sub + 1)],
                                 psA[:, 512 * ti + 16 * sub: 512 * ti + 16 * (sub + 1)],
                                 AF.Copy, scale=0.5)
            se.inc(mi_auto)
        Sb(f_dqev2)

        def f_dqst(g):
            se.wait(g)
            gsp.begin(g)
            for ti in range(4):
                for sub in range(4):
                    gsp.dma(g, d_dqT[4 * ti + sub], dgrot[:, ti, 16 * sub: 16 * (sub + 1)])
            rec["dqst"] = gsp.tok()
        Gb(f_dqst)

        if _stop("s1dq"):
            return nc, tap

        def f_wk(g):
            tm.wait(g)
            glp.begin(g)
            glp.dma(g, wk[:, :, :], wkp)
            glp.dma(g, brf[0:1, 0:512], bk_row)
            glp.dma(g, brf[0:1, 512:1024], bv_row)
            rec["wk"] = glp.tok()
        Gb(f_wk)

        def f_bkprep(v):
            tw(v, rec["wk"])
            v.tensor_copy(bkb[:], brf[0:1, 0:512])
            v.tensor_copy(bvb2[:], brf[0:1, 512:1024])
            v.tensor_tensor(bklo[:], brf[0:1, 0:512], bkb[:], ALU.subtract)
            ve.inc(v.tensor_tensor(bvlo[:], brf[0:1, 512:1024], bvb2[:], ALU.subtract))
        Vb(f_bkprep)

        for grp in range(2):
            for fch in range(4):
                for tih in range(2):
                    def f_kmm(t, grp=grp, fch=fch, tih=tih):
                        tw(t, rec["wk"])
                        if grp == 0 and fch == 0 and tih == 0:
                            ve.wait(t)
                        if fch:
                            se.wait(t, rec[("kev", grp, fch - 1, tih)])
                            ve.wait(t, rec[("kcv", grp, fch - 1, tih)])
                        elif tih == 0:
                            se.wait(t)
                            ve.wait(t)
                        else:
                            se.wait(t)
                            ve.wait(t)
                        for ech in range(8):
                            for ti in (2 * tih, 2 * tih + 1):
                                col = 2048 * grp + 512 * ti
                                t.matmul(psA[:, 512 * ti: 512 * (ti + 1)],
                                         wk[:, ech, 128 * fch: 128 * (fch + 1)],
                                         ln1T[:, ech, col: col + 512],
                                         start=(ech == 0), stop=False)
                        for ti in (2 * tih, 2 * tih + 1):
                            t.matmul(psA[:, 512 * ti: 512 * (ti + 1)],
                                     bkb[0:1, 128 * fch: 128 * (fch + 1)], ones4[0:1, :],
                                     start=False, stop=False)
                            mi_auto = t.matmul(psA[:, 512 * ti: 512 * (ti + 1)],
                                     bklo[0:1, 128 * fch: 128 * (fch + 1)], ones4[0:1, :],
                                     start=False, stop=True)
                        rec[("kmm", grp, fch, tih)] = tm.n + 1
                        tm.inc(mi_auto)
                    Tb(f_kmm)

                    def f_kcv(v, grp=grp, fch=fch, tih=tih):
                        tm.wait(v, rec[("kmm", grp, fch, tih)])
                        for ti in (2 * tih, 2 * tih + 1):
                            col = 2048 * grp + 512 * ti
                            mi_auto = v.tensor_scalar(kp[:, fch, col: col + 512],
                                                      psA[:, 512 * ti: 512 * (ti + 1)],
                                                      S2C, None, ALU.mult)
                        rec[("kcv", grp, fch, tih)] = ve.n + 1
                        ve.inc(mi_auto)
                    Vb(f_kcv)

                    def f_kev(s, grp=grp, fch=fch, tih=tih):
                        tm.wait(s, rec[("kmm", grp, fch, tih)])
                        for ti in (2 * tih, 2 * tih + 1):
                            mi_auto = s.activation(sqrot[:, ti, :],
                                                   psA[:, 512 * ti: 512 * (ti + 1)],
                                                   AF.Square, scale=S2C)
                        rec[("kev", grp, fch, tih)] = se.n + 1
                        se.inc(mi_auto)
                    Sb(f_kev)

                def f_kdg(t, fch=fch):
                    se.wait(t)
                    for ti in range(4):
                        mi_auto = t.matmul(psA[0:8, 2048 + 512 * ti: 2048 + 512 * ti + 512],
                                 selk[:, 8 * fch: 8 * (fch + 1)], sqrot[:, ti, :],
                                 start=(fch == 0), stop=(fch == 3), skip_group_check=True)
                    tm.inc(mi_auto)
                Tb(f_kdg)

            def f_kdev(s):
                tm.wait(s)
                tw(s, rec["dqst"])
                for ti in range(4):
                    mi_auto = s.copy(dgrot[0:8, ti, :], psA[0:8, 2048 + 512 * ti: 2048 + 512 * ti + 512])
                se.inc(mi_auto)
            Sb(f_kdev)

            def f_kdtr(t):
                se.wait(t)
                for ti in range(4):
                    for sub in range(4):
                        mi_auto = t.transpose(psA[:, 512 * ti + 16 * sub: 512 * ti + 16 * sub + 8],
                                    dgrot[0:8, ti, 128 * sub: 128 * (sub + 1)], ident[0:8, 0:8])
                tm.inc(mi_auto)
            Tb(f_kdtr)

            def f_kded(s, grp=grp):
                tm.wait(s)
                for ti in range(4):
                    for sub in range(4):
                        kc = 16 * grp + 4 * ti + sub
                        mi_auto = s.activation(edk_sb[:, kc, :],
                                     psA[:, 512 * ti + 16 * sub: 512 * ti + 16 * sub + 8],
                                     AF.Exp, scale=-0.5)
                se.inc(mi_auto)
            Sb(f_kded)

        if _stop("s1k"):
            return nc, tap

        def f_wv(g):
            tm.wait(g)
            glp.begin(g)
            glp.dma(g, wv[:, :, :], wvp)
            rec["wv"] = glp.tok()
        Gb(f_wv)

        for n2 in range(KCH):
            def f_vmm(t, n2=n2):
                tw(t, rec["wv"])
                if n2 == 0:
                    se.wait(t)
                    ve.wait(t)
                if n2 >= 2:
                    ve.wait(t, rec[("vev", n2 - 2)])
                bi = n2 % 2
                for ech in range(8):
                    t.matmul(psA[:, 512 * bi: 512 * (bi + 1)],
                             ln1T[:, ech, 128 * n2: 128 * (n2 + 1)],
                             wv[:, ech, :], start=(ech == 0), stop=False)
                t.matmul(psA[:, 512 * bi: 512 * (bi + 1)],
                         onesr[0:1, 0:128], bvb2[0:1, :], start=False, stop=False)
                mi_auto = t.matmul(psA[:, 512 * bi: 512 * (bi + 1)],
                         onesr[0:1, 0:128], bvlo[0:1, :], start=False, stop=True)
                tm.inc(mi_auto)
            Tb(f_vmm)

            def f_vev(v, n2=n2):
                tm.wait(v)
                if n2 >= 2:
                    tw(v, rec[("vst", n2 - 2)])
                rec[("vev", n2)] = ve.n + 1
                ve.inc(v.tensor_copy(vrot[:, n2 % 2, :],
                                     psA[:, 512 * (n2 % 2): 512 * (n2 % 2) + 512]))
            Vb(f_vev)

            def f_vs(t, n2=n2):
                ve.wait(t)
                mi_auto = t.matmul(psA[0:1, 1024:1536], onesc[:, 0:1], vrot[:, n2 % 2, :],
                         start=(n2 == 0), stop=(n2 == KCH - 1), skip_group_check=True)
                tm.inc(mi_auto)
            Tb(f_vs)

            def f_vst(g, n2=n2):
                ve.wait(g)
                gsp.begin(g)
                gsp.dma(g, d_vtok[n2], vrot[:, n2 % 2, :])
                rec[("vst", n2)] = gsp.tok()
            Gb(f_vst)

        def f_vsev(v):
            tm.wait(v)
            ve.inc(v.tensor_copy(vss[0:1, :], psA[0:1, 1024:1536]))
        Vb(f_vsev)

        if _stop("s1v"):
            return nc, tap

        for tn, tsrc, cnt in (("qp", d_qp, 8), ("vtok", d_vtok, KCH), ("dqT", d_dqT, NCH)):
            if tn in tap:
                def t_d(g, tn=tn, tsrc=tsrc, cnt=cnt):
                    gsp.wait_all(g)
                    gsp.begin(g)
                    for i in range(cnt):
                        gsp.dma(g, tap[tn][i], tsrc[i])
                Gb(t_d)
        if "kp" in tap:
            def t_kp(g):
                se.wait(g)
                gsp.begin(g)
                for i in range(4):
                    gsp.dma(g, tap["kp"][i], kp[:, i, :])
            Gb(t_kp)
        if "edk" in tap:
            def t_edk(g):
                se.wait(g)
                gsp.begin(g)
                gsp.dma(g, tap["edk"][:], edk_sb[:])
            Gb(t_edk)

        if _stop("s1"):
            return nc, tap

        def f_km0(v):
            tm.wait(v)
            ve.inc(v.memset(kmaxt[:], 0.0))
        Vb(f_km0)

        def f_gclr(t):
            se.wait(t)
            ve.wait(t)
            tw(t, rec["consts"])
            for b2 in range(4):
                mi_auto = t.matmul(psA[0:65, 512 * b2: 512 * (b2 + 1)], zbf[0:1, 0:65],
                         ones4[0:1, :], start=True, stop=False, skip_group_check=True)
            tm.inc(mi_auto)
        Tb(f_gclr)

        _kcn = {"s2a0": 0, "s2a1": 1, "s2a2": 1, "s2a": 2, "s2v1": 1, "s2v2": 1,
                "s2v3": 1, "s2v4": 1, "s2w1": 1, "s2w2": 1}.get(_STOP, KCH)
        _s2l = {"s2v1": 1, "s2v2": 2, "s2v3": 3, "s2v4": 4,
                "s2w1": 3, "s2w2": 3}.get(_STOP, 9)
        for kc in range(_kcn):
            def f_vtl(g, kc=kc):
                tw(g, rec[("vst", kc)])
                if kc == 0:
                    tm.wait(g)
                if kc >= 2:
                    ve.wait(g, rec[("vvb", kc - 2)])
                glp.begin(g)
                glp.dma(g, vtrot[:, kc % 2, :], d_vtok[kc])
                rec[("vtl", kc)] = glp.tok()
            Gb(f_vtl)

            def f_vvb(v, kc=kc):
                tw(v, rec[("vtl", kc)])
                if kc == 0:
                    se.wait(v)
                if kc >= 2:
                    tm.wait(v, rec[("gmm", kc - 2)])
                s = kc % 2
                for h in range(8):
                    v.tensor_scalar(vvrot[:, s, h, 0:64], vtrot[:, s, 64 * h: 64 * (h + 1)],
                                    edk_sb[:, kc, h: h + 1], None, ALU.mult)
                rec[("vvb", kc)] = ve.n + 1
                ve.inc(v.tensor_copy(
                    vvrot[:, s, :, 64:65].rearrange("p a b -> p (a b)"),
                    edk_sb[:, kc, :]))
            if _s2l < 2:
                continue
            Vb(f_vvb)

            for half in range(2):
                def f_dd(t, kc=kc, half=half):
                    if kc == 0:
                        se.wait(t)
                        ve.wait(t)
                    if kc > 0:
                        se.wait(t, rec[("exp", kc - 1, half)])
                    for fch in (2 * half, 2 * half + 1):
                        base = 2048 + 1024 * half + 512 * (fch % 2)
                        mi_auto = t.matmul(psA[:, base: base + 512],
                                 kp[:, fch, 128 * kc: 128 * (kc + 1)],
                                 projT[:, :], start=True, stop=True)
                    tm.inc(mi_auto)
                if _s2l < 3:
                    continue
                Tb(f_dd)

                def f_exp(s, kc=kc, half=half):
                    tm.wait(s)
                    if kc >= 2:
                        ve.wait(s, rec[("kmx", kc - 2)])
                        tm.wait(s, rec[("gmm", kc - 2)])
                    rec[("exp", kc, half)] = se.n + 1
                    se.inc(s.activation(
                        ekrot[:, kc % 2, 4 * half: 4 * half + 4, :].rearrange("p a b -> p (a b)"),
                        psA[:, 2048 + 1024 * half: 2048 + 1024 * (half + 1)],
                        AF.Exp))
                if _s2l < 4:
                    continue
                Sb(f_exp)

            def f_kmx(v, kc=kc):
                se.wait(v)
                s = kc % 2
                for i in range(4):
                    mi_auto = v.tensor_tensor(kmaxt[:], kmaxt[:],
                                    ekrot[:, s, 2 * i: 2 * i + 2, :].rearrange("p a b -> p (a b)"),
                                    ALU.max)
                rec[("kmx", kc)] = ve.n + 1
                ve.inc(mi_auto)
            if _s2l < 5:
                continue
            Vb(f_kmx)

            def f_g(t, kc=kc):
                ve.wait(t)
                for h in range(8):
                    mi_auto = t.matmul(psA[0:65, 256 * h: 256 * (h + 1)],
                             vvrot[:, kc % 2, h, 0:65],
                             ekrot[:, kc % 2, h, :],
                             start=False, stop=(kc == KCH - 1), skip_group_check=True)
                rec[("gmm", kc)] = tm.n + 1
                tm.inc(mi_auto)
            if _STOP != "s2a1":
                Tb(f_g)

        if _STOP in ("s2a0", "s2a1", "s2v1", "s2v2", "s2v3", "s2v4",
                     "s2w1", "s2w2"):
            def f_dbg2(g):
                tm.wait(g)
                ve.wait(g)
                se.wait(g)
            Gb(f_dbg2)
        if (_stop("s2a") or _stop("s2b") or _stop("s2a0") or _stop("s2a1")
                or _stop("s2a2") or _stop("s2v1") or _stop("s2v2") or _stop("s2v3")
                or _stop("s2v4") or _stop("s2w1") or _stop("s2w2")):
            return nc, tap

        def f_gev(s):
            tm.wait(s)
            for h in range(8):
                mi_auto = s.copy(gx[0:65, 256 * h: 256 * (h + 1)], psA[0:65, 256 * h: 256 * (h + 1)])
            se.inc(mi_auto)
        Sb(f_gev)

        if _stop("s2c"):
            return nc, tap

        def f_kmf1(v):
            ve.wait(v)
            ve.inc(v.tensor_reduce(sc1[:, 0:1], kmaxt[:], AX.X, ALU.max))
        Vb(f_kmf1)

        def f_kmf2(t):
            ve.wait(t)
            se.wait(t)
            mi_auto = t.transpose(psA[0:1, 2048:2176], sc1[:, 0:1], ident[:])
            tm.inc(mi_auto)
        Tb(f_kmf2)

        def f_kmf3(v):
            tm.wait(v)
            mi_auto = v.tensor_reduce(srow[0:1, 0:1], psA[0:1, 2048:2176], AX.X, ALU.max)
            ve.inc(mi_auto)
        Vb(f_kmf3)

        def f_kmf4(s):
            ve.wait(s)
            se.inc(s.activation(srow[0:1, 8:9], srow[0:1, 0:1], AF.Ln))
        Sb(f_kmf4)

        def f_kmf5(v):
            se.wait(v)
            for i in range(1, 8):
                v.tensor_copy(srow[0:1, 8 + i: 9 + i], srow[0:1, 8:9])
            mi_auto = v.memset(zrow[:], 0.0)
            ve.inc(mi_auto)
        Vb(f_kmf5)

        if _stop("s2"):
            return nc, tap

        def f_ag(g):
            se.wait(g)
            ve.wait(g)
            gsp.begin(g)
            for h in range(8):
                gsp.dma(g, ag_in.ap()[h, 0:65, :], gx[0:65, 256 * h: 256 * (h + 1)])
                gsp.dma(g, ag_in.ap()[h, 65:66, 0:64], vss[0:1, 64 * h: 64 * (h + 1)])
                gsp.dma(g, ag_in.ap()[h, 65:66, 64:256], zrow[0:1, 0:192])
            gsp.dma(g, mxk_in.ap()[:, :], srow[0:1, 8:16])
            tw(g, gsp.tok())
            cc.inc(g.collective_compute(
                "AllGather", ALU.bypass,
                replica_groups=[[0, 1], [2, 3], [4, 5], [6, 7]],
                ins=[ag_in.ap().opt()], outs=[ag_out.ap().opt()]))
            cc.inc(g.collective_compute(
                "AllReduce", ALU.max,
                replica_groups=[[0, 1, 2, 3, 4, 5, 6, 7]],
                ins=[mxk_in.ap().opt()], outs=[mxk_out.ap().opt()]))
            g.wait_ge(cc.h, cc.n)
            glp.begin(g)
            for h in range(16):
                glp.dma(g, gf[0:65, 256 * h: 256 * (h + 1)], ag_out.ap()[h, 0:65, :])
                glp.dma(g, vsr[0:1, 64 * h: 64 * (h + 1)], ag_out.ap()[h, 65:66, 0:64])
            glp.dma(g, srow[0:1, 16:24], mxk_out.ap()[:, :])
            rec["ex_done"] = glp.tok()
        Gb(f_ag)

        if "gx" in tap:
            def t_gx(g):
                cc.wait(g)
                gsp.begin(g)
                for h in range(8):
                    gsp.dma(g, tap["gx"][h], ag_in.ap()[h])
            Gb(t_gx)
        if "gf" in tap:
            def t_gf(g):
                cc.wait(g)
                gsp.begin(g)
                for h in range(16):
                    gsp.dma(g, tap["gf"][h], ag_out.ap()[h])
            Gb(t_gf)
        if "mxk" in tap:
            def t_mxk(g):
                cc.wait(g)
                gsp.begin(g)
                gsp.dma(g, tap["mxk"][:, :], mxk_out.ap()[:, :])
            Gb(t_mxk)

        if _stop("ex"):
            return nc, tap

        def f_dqld(g):
            se.wait(g)
            tw(g, rec["dqst"])
            ve.wait(g)
            glp.begin(g)
            glp.dma(g, dq_all[:, :, :], d_dqT.rearrange("c p j -> p c j"))
            rec["dqld"] = glp.tok()
        Gb(f_dqld)

        def emit_dd_half(half):
            for i in range(4):
                fc = 4 * half + i

                def f_qpl(g, fc=fc, i=i, half=half):
                    if half == 0 and i == 0:
                        tm.wait(g)
                    tw(g, rec[("qst", fc)])
                    if i >= 2 or half == 1:
                        prev = fc - 2
                        se.wait(g, rec[("eqx", prev)])
                    glp.begin(g)
                    glp.dma(g, qprot[:, fc % 2, :], d_qp[fc])
                    rec[("qpl", fc)] = glp.tok()
                Gb(f_qpl)

                for sub in range(2):
                    h = 2 * fc + sub
                    hl = h - 8 * half
                    for blkk in range(2):
                        def f_ddq(t, fc=fc, sub=sub, h=h, blkk=blkk):
                            tw(t, rec[("qpl", fc)])
                            tw(t, rec["dqld"])
                            if h % 8 == 0 and blkk == 0:
                                ve.wait(t)
                            se.wait(t)
                            for n1 in range(8 * blkk, 8 * blkk + 8):
                                slot = n1 % 8
                                mi_auto = t.matmul(psA[:, 256 * slot: 256 * (slot + 1)],
                                         qprot[64 * sub: 64 * sub + 64, fc % 2,
                                               128 * n1: 128 * (n1 + 1)],
                                         projT[64 * sub: 64 * sub + 64,
                                               256 * sub: 256 * sub + 256],
                                         start=True, stop=True)
                            rec[("ddq", h, blkk)] = tm.n + 1
                            tm.inc(mi_auto)
                        Tb(f_ddq)

                        def f_stab(v, h=h, blkk=blkk):
                            tm.wait(v, rec[("ddq", h, blkk)])
                            se.wait(v)
                            for n1 in range(8 * blkk, 8 * blkk + 8):
                                slot = n1 % 8
                                v.tensor_reduce(ep1[:, slot: slot + 1],
                                                psA[:, 256 * slot: 256 * (slot + 1)],
                                                AX.X, ALU.max, negate=True)
                            for n1 in range(8 * blkk, 8 * blkk + 8):
                                slot = n1 % 8
                                mi_auto = v.tensor_tensor(ep2[:, slot: slot + 1],
                                                ep1[:, slot: slot + 1],
                                                dq_all[:, n1, h: h + 1], ALU.subtract)
                            ve.inc(mi_auto)
                        Vb(f_stab)

                        def f_eqx(s, h=h, hl=hl, fc=fc, blkk=blkk, half=half):
                            ve.wait(s)
                            if hl == 0 and blkk == 0:
                                tm.wait(s)
                            if blkk == 0 and hl >= 1:
                                ve.wait(s, rec[("eqtr", hl - 1, half)])
                            for n1 in range(8 * blkk, 8 * blkk + 8):
                                slot = n1 % 8
                                mi_auto = s.activation(eqrot[:, 0, n1, :],
                                             psA[:, 256 * slot: 256 * (slot + 1)],
                                             AF.Exp, bias=ep2[:, slot: slot + 1])
                            rec[("eqx", fc)] = se.n + 1
                            se.inc(mi_auto)
                        Sb(f_eqx)

                    for wv in range(2):
                        par = (2 * hl + wv) % 2
                        def f_eqt_t(t, hl=hl, half=half, wv=wv, par=par):
                            se.wait(t)
                            if ("eqte", half, par) in rec:
                                ve.wait(t, rec[("eqte", half, par)])
                            base = 4096 + 2048 * par
                            k = 0
                            for n1 in range(8 * wv, 8 * wv + 8):
                                for mc in range(2):
                                    mi_auto = t.transpose(
                                        psAb[:, base + 128 * k: base + 128 * (k + 1)],
                                        eqrot[:, 0, n1, 128 * mc: 128 * (mc + 1)],
                                        identb[:])
                                    k += 1
                            tm.inc(mi_auto)
                        Tb(f_eqt_t)

                        def f_eqt_e(v, hl=hl, half=half, wv=wv, par=par):
                            tm.wait(v)
                            base = 4096 + 2048 * par
                            k = 0
                            for n1 in range(8 * wv, 8 * wv + 8):
                                for mc in range(2):
                                    mi_auto = v.tensor_copy(
                                        eqT[:, hl, mc, 128 * n1: 128 * (n1 + 1)],
                                        psAb[:, base + 128 * k: base + 128 * (k + 1)])
                                    k += 1
                            rec[("eqte", half, par)] = ve.n + 1
                            ve.inc(mi_auto)
                        Vb(f_eqt_e)
                    rec[("eqtr", hl, half)] = rec[("eqte", half, (2 * hl + 1) % 2)]

        emit_dd_half(0)

        if _stop("s4a"):
            return nc, tap

        def f_et(s):
            tw(s, rec["ex_done"])
            se.inc(s.activation(srow[0:1, 17:18], srow[0:1, 16:17], AF.Exp))
        Sb(f_et)

        def f_rows(v):
            se.wait(v)
            v.tensor_scalar(srow[0:1, 18:19], srow[0:1, 17:18],
                            float(KEPS) * float(NSEQ * KEPS * M), None, ALU.mult)
            v.tensor_scalar(srow[0:1, 19:20], srow[0:1, 17:18],
                            float(KEPS) * float(NSEQ), None, ALU.mult)
            mi_auto = v.tensor_scalar(srow[0:1, 17:18], srow[0:1, 17:18], KEPS, None, ALU.mult)
            ve.inc(mi_auto)
        Vb(f_rows)

        for h in range(16):
            for mc2 in range(2):
                i = 2 * h + mc2
                cb = 1536 + 512 * (i % 2)

                def f_ctr(t, h=h, mc2=mc2, i=i, cb=cb):
                    if i == 0:
                        tw(t, rec["ex_done"])
                        ve.wait(t)
                        se.wait(t)
                    if i >= 2:
                        ve.wait(t, rec[("cbe", i - 2)])
                    mi_auto = t.transpose(psA[:, cb: cb + 65],
                                gf[0:65, 256 * h + 128 * mc2: 256 * h + 128 * (mc2 + 1)],
                                ident[0:65, 0:65])
                    tm.inc(mi_auto)
                Tb(f_ctr)

                def f_cbe(v, h=h, mc2=mc2, i=i, cb=cb):
                    tm.wait(v)
                    v.tensor_copy(caug[:, mc2, h, 0:65], psA[:, cb: cb + 65])
                    mi_auto = v.memset(caug[:, mc2, h, 65:66], 1.0)
                    rec[("cbe", i)] = ve.n + 1
                    ve.inc(mi_auto)
                Vb(f_cbe)

        for h in range(16):
            def f_csmm(t, h=h):
                ve.wait(t)
                for mc2 in range(2):
                    mi_auto = t.matmul(psA[0:1, 1280:1346], onesc[:, 0:1], caug[:, mc2, h, :],
                             start=(mc2 == 0), stop=(mc2 == 1), skip_group_check=True)
                tm.inc(mi_auto)
            Tb(f_csmm)

            def f_csev(v, h=h):
                tm.wait(v)
                ve.inc(v.tensor_copy(csum[0:1, h, :], psA[0:1, 1280:1346]))
            Vb(f_csev)

        def f_kser(v):
            ve.wait(v)
            v.tensor_scalar(jrow[:], jrow[:], 0.0, None, ALU.mult)
            ve.inc(v.tensor_scalar(srow[0:1, 32:48],
                                   csum[0:1, :, 64:65].rearrange("p a b -> p (a b)"),
                                   KEPS, srow[0:1, 18:19], ALU.mult, ALU.add))
        Vb(f_kser)

        def f_bc(t):
            ve.wait(t)
            t.matmul(psA[:, 0:512], onesf[0:1, :], vsr[0:1, 0:512], start=True, stop=True)
            t.matmul(psA[:, 512:1024], onesf[0:1, :], vsr[0:1, 512:1024], start=True, stop=True)
            t.matmul(psA[:, 1024:1040], onesf[0:1, :], srow[0:1, 32:48], start=True, stop=True)
            t.matmul(psA[:, 1040:1041], onesf[0:1, :], srow[0:1, 19:20], start=True, stop=True)
            mi_auto = t.matmul(psA[:, 1041:1042], onesf[0:1, :], srow[0:1, 17:18], start=True, stop=True)
            tm.inc(mi_auto)
        Tb(f_bc)

        def f_bcev(v):
            tm.wait(v)
            vf = vs2e.ap().rearrange("p h d -> p (h d)")
            v.tensor_copy(scm[:, 2:3], psA[:, 1041:1042])
            v.tensor_copy(kse[:], psA[:, 1024:1040])
            v.tensor_scalar(vf[:, 0:512], psA[:, 0:512], scm[:, 2:3], None, ALU.mult)
            v.tensor_scalar(vf[:, 512:1024], psA[:, 512:1024], scm[:, 2:3], None, ALU.mult)
            mi_auto = v.tensor_copy(encol[:], psA[:, 1040:1041])
            rec["cready"] = ve.n + 1
            ve.inc(mi_auto)
        Vb(f_bcev)

        if "caug" in tap:
            def t_ca(g):
                ve.wait(g)
                gsp.begin(g)
                gsp.dma(g, tap["caug"][:], caug[:])
            Gb(t_ca)

        if _stop("s3c"):
            return nc, tap

        _epd = {}
        if "epd" in tap:
            _epd["sb"] = nc.alloc_sbuf_tensor_at("epd_sb", [128, 1048], F32,
                                                 offset=OB + 139264)
        def _ep_tail(v, n1p, half):
            if n1p >= 2:
                se.wait(v, rec[("atr", half, n1p - 2)])
            pbp = 2048 + 1024 * (n1p % 2)
            pvp = psA[:, pbp: pbp + 1024].rearrange("p (h c) -> p h c", h=8)
            v.tensor_tensor(psA[:, 0:512].rearrange("p (h c) -> p h c", h=8),
                            c1s.ap().rearrange("p (h c) -> p h c", h=8),
                            pvp[:, :, 0:64], ALU.add)

        def _ep_att(v, n1p, half):
            for hl in range(8):
                mi_auto = v.tensor_scalar(att_rot[:, n1p % 2, 64 * hl: 64 * (hl + 1)],
                                          psA[:, 64 * hl: 64 * (hl + 1)],
                                          ep3[:, hl: hl + 1], None, ALU.mult)
            rec[("epi", half, n1p)] = ve.n + 1
            ve.inc(mi_auto)

        def emit_p_half(half):
            def f_atr_t(t, n1, half=half):
                ve.wait(t, rec[("epi", half, n1)])
                if ("atre", half, n1 - 2) in rec:
                    se.wait(t, rec[("atre", half, n1 - 2)])
                base = 1024 + 512 * (n1 % 2)
                for jj in range(4):
                    mi_auto = t.transpose(
                        psAb[:, base + 128 * jj: base + 128 * (jj + 1)],
                        att_rot[:, n1 % 2, 128 * jj: 128 * (jj + 1)], identb[:])
                tm.inc(mi_auto)

            def f_atr_e(s, n1, half=half):
                tm.wait(s)
                base = 1024 + 512 * (n1 % 2)
                for jj in range(4):
                    mi_auto = s.copy(attn_fm[:, 4 * half + jj, 128 * n1: 128 * (n1 + 1)],
                                     psAb[:, base + 128 * jj: base + 128 * (jj + 1)])
                rec[("atr", half, n1)] = se.n + 1
                rec[("atre", half, n1)] = se.n + 1
                se.inc(mi_auto)

            for n1 in range(NCH):
                def f_pmm(t, n1=n1, half=half):
                    if n1 == 0:
                        ve.wait(t, rec["cready"])
                        slp.wait_all(t)
                        ve.wait(t)
                        se.wait(t)
                    if n1 >= 2:
                        ve.wait(t, rec[("epi", half, n1 - 2)])
                    pb = 2048 + 1024 * (n1 % 2)
                    for hl in range(8):
                        hg = 8 * half + hl
                        for mc2 in range(2):
                            t.matmul(psA[:, pb + 128 * hl: pb + 128 * hl + 66],
                                     eqT[:, hl, mc2, 128 * n1: 128 * (n1 + 1)],
                                     caug[:, mc2, hg, :], start=(mc2 == 0), stop=False,
                                     skip_group_check=True)
                        mi_auto = t.matmul(psA[:, pb + 128 * hl: pb + 128 * hl + 66],
                                 epsr[0:1, 0:128], csum[0:1, hg, :],
                                 start=False, stop=True, skip_group_check=True)
                    rec[("pmm", half, n1)] = tm.n + 1
                    tm.inc(mi_auto)
                Tb(f_pmm)

                def f_epi(v, n1=n1, half=half):
                    tm.wait(v, rec[("pmm", half, n1)])
                    if n1 == 0 and half == 0 and "epd" in tap:
                        v.tensor_copy(_epd["sb"][:, 0:1024], psA[:, 2048:3072])
                    if n1 == 0:
                        for _ in range(4):
                            v.tensor_copy(c1s[:, 0:512], vs2e.ap().rearrange(
                                "p h d -> p (h d)")[:, 0:512])
                    pb = 2048 + 1024 * (n1 % 2)
                    pv = psA[:, pb: pb + 1024].rearrange("p (h c) -> p h c", h=8)
                    v.tensor_copy(ep1[:], pv[:, :, 65:66].rearrange("p a b -> p (a b)"))
                    if n1 == 0:
                        v.tensor_scalar(jrow[:], jrow[:], 0.0, None, ALU.mult)
                    else:
                        _ep_tail(v, n1 - 1, half)
                    v.scalar_tensor_tensor(ep2[:], ep1[:], encol[:, 0:1],
                                           pv[:, :, 64:65].rearrange("p a b -> p (a b)"),
                                           ALU.mult, ALU.add)
                    if n1 > 0:
                        _ep_att(v, n1 - 1, half)
                    for hl in range(8):
                        v.tensor_scalar(c1s[:, 64 * hl: 64 * (hl + 1)],
                                        vs2e[:, 8 * half + hl, :], ep1[:, hl: hl + 1],
                                        None, ALU.mult)
                    mi_auto = v.reciprocal(ep3[:], ep2[:])
                    if n1 == 0 and half == 0 and "epd" in tap:
                        v.tensor_copy(_epd["sb"][:, 1024:1032], ep1[:])
                        v.tensor_copy(_epd["sb"][:, 1032:1040], ep2[:])
                        mi_auto = v.tensor_copy(_epd["sb"][:, 1040:1048], ep3[:])
                    ve.inc(mi_auto)
                Vb(f_epi)

                if n1 >= 1:
                    def f_atrpt(t, n1=n1, half=half):
                        f_atr_t(t, n1 - 1)
                    Tb(f_atrpt)

                    def f_atrpe(s, n1=n1, half=half):
                        f_atr_e(s, n1 - 1)
                    Sb(f_atrpe)

            def f_epit(v, half=half):
                _ep_tail(v, NCH - 1, half)
                v.tensor_scalar(jrow[:], jrow[:], 0.0, None, ALU.mult)
                _ep_att(v, NCH - 1, half)
            Vb(f_epit)

            def f_atrlt(t, half=half):
                f_atr_t(t, NCH - 1)
            Tb(f_atrlt)

            def f_atrle(s, half=half):
                f_atr_e(s, NCH - 1)
            Sb(f_atrle)

        emit_p_half(0)
        if "epd" in tap:
            def t_epd(g):
                ve.wait(g)
                gsp.begin(g)
                gsp.dma(g, tap["epd"][:], _epd["sb"][:])
            Gb(t_epd)
        emit_dd_half(1)
        emit_p_half(1)

        if "attn" in tap:
            def t_at(g):
                slp.wait_all(g)
                gsp.begin(g)
                gsp.dma(g, tap["attn"][:], attn_fm[:])
            Gb(t_at)
        if "eqt" in tap:
            def t_eqt(g):
                slp.wait_all(g)
                gsp.begin(g)
                gsp.dma(g, tap["eqt"][:], eqT[:])
            Gb(t_eqt)

        if _stop("s4p"):
            return nc, tap

        wop = woTd.rearrange("(ec p) f -> p ec f", p=128)

        def f_wol(g):
            tm.wait(g)
            slp.wait_all(g)
            ve.wait(g)
            glp.begin(g)
            glp.dma(g, woTs[:, :, :], wop)
            glp.dma(g, brf5[:], bo_row)
            rec["wol"] = glp.tok()
        Gb(f_wol)

        def f_boprep(v):
            tw(v, rec["wol"])
            v.tensor_copy(bob2[:], brf5[:])
            v.tensor_scalar(jrow[:], jrow[:], 0.0, None, ALU.mult)
            ve.inc(v.tensor_tensor(bolo[:], brf5[:], bob2[:], ALU.subtract))
        Vb(f_boprep)

        for n1 in range(NCH):
            j = n1 % 4

            def f_womm(t, n1=n1):
                tw(t, rec["wol"])
                if n1 == 0:
                    se.wait(t)
                    ve.wait(t)
                if n1 >= 2:
                    ve.wait(t, rec[("xm", n1 - 2)])
                pb = 1024 * (n1 % 2)
                for ech in range(8):
                    for fh2 in range(2):
                        t.matmul(psA[:, pb + 512 * fh2: pb + 512 * (fh2 + 1)],
                                 attn_fm[:, ech, 128 * n1: 128 * (n1 + 1)],
                                 woTs[:, ech, 512 * fh2: 512 * (fh2 + 1)],
                                 start=(ech == 0), stop=False, skip_group_check=True)
                for fh2 in range(2):
                    t.matmul(psA[:, pb + 512 * fh2: pb + 512 * (fh2 + 1)],
                             onesr[0:1, 0:128], bob2[0:1, 512 * fh2: 512 * (fh2 + 1)],
                             start=False, stop=False, skip_group_check=True)
                    mi_auto = t.matmul(psA[:, pb + 512 * fh2: pb + 512 * (fh2 + 1)],
                             onesr[0:1, 0:128], bolo[0:1, 512 * fh2: 512 * (fh2 + 1)],
                             start=False, stop=True, skip_group_check=True)
                rec[("wo", n1)] = tm.n + 1
                tm.inc(mi_auto)
            Tb(f_womm)

            def f_xm(v, n1=n1):
                tm.wait(v, rec[("wo", n1)])
                pb = 1024 * (n1 % 2)
                rec[("xm", n1)] = ve.n + 1
                ve.inc(v.tensor_tensor(xmid[:, n1, :], psA[:, pb: pb + 1024],
                                       xa[:, n1, :], ALU.add))
                if n1 == 8:
                    ve.inc(v.tensor_reduce(stat2[:, 4:5], stat[:, 8:16], AX.X, ALU.max))
            Vb(f_xm)

            if n1 % 2 == 0:
                continue
            n0 = n1 - 1
            j0, j1 = n0 % 4, n1 % 4

            if _STOP == "s5a":
                continue

            def f_l2v1(v, n0=n0, n1=n1, j0=j0, j1=j1):
                v.tensor_reduce(stat[:, j0: j0 + 1], xmid[:, n0, :], AX.X, ALU.add)
                v.tensor_reduce(stat[:, j1: j1 + 1], xmid[:, n1, :], AX.X, ALU.add)
                v.tensor_scalar(stat[:, 4 + j0: 5 + j0], stat[:, j0: j0 + 1], 1.0 / E, None, ALU.mult)
                v.tensor_scalar(stat[:, 4 + j1: 5 + j1], stat[:, j1: j1 + 1], 1.0 / E, None, ALU.mult)
                v.tensor_tensor(stat2[:, j0: j0 + 1], stat[:, 4 + j0: 5 + j0],
                                stat[:, 4 + j0: 5 + j0], ALU.mult)
                v.tensor_tensor(stat2[:, j1: j1 + 1], stat[:, 4 + j1: 5 + j1],
                                stat[:, 4 + j1: 5 + j1], ALU.mult)
                v.tensor_scalar(stat2[:, j0: j0 + 1], stat2[:, j0: j0 + 1], -1.0, LNEPS,
                                ALU.mult, ALU.add)
                ve.inc(v.tensor_scalar(stat2[:, j1: j1 + 1], stat2[:, j1: j1 + 1], -1.0, LNEPS,
                                       ALU.mult, ALU.add))
            Vb(f_l2v1)

            def f_l2s1(s, n0=n0, n1=n1, j0=j0, j1=j1):
                ve.wait(s)
                if n0 == 0:
                    tm.wait(s)
                s.activation(qtrot[:, 0, :], xmid[:, n0, :], AF.Square,
                             accum_out=scm[:, j0: j0 + 1])
                s.activation(qtrot[:, 1, :], xmid[:, n1, :], AF.Square,
                             accum_out=scm[:, j1: j1 + 1])
                s.activation(sc1[:, 0:1], scm[:, j0: j0 + 1], AF.Sqrt,
                             scale=1.0 / E, bias=stat2[:, j0: j0 + 1])
                se.inc(s.activation(sc1[:, 1:2], scm[:, j1: j1 + 1], AF.Sqrt,
                                    scale=1.0 / E, bias=stat2[:, j1: j1 + 1]))
            Sb(f_l2s1)

            def f_l2v2(v, j0=j0, j1=j1):
                se.wait(v)
                v.reciprocal(sc2[:, 0:1], sc1[:, 0:1])
                v.reciprocal(sc2[:, 1:2], sc1[:, 1:2])
                v.tensor_scalar(sc3[:, j0: j0 + 1], stat[:, 4 + j0: 5 + j0], sc2[:, 0:1],
                                -1.0, ALU.mult, ALU.mult)
                ve.inc(v.tensor_scalar(sc3[:, j1: j1 + 1], stat[:, 4 + j1: 5 + j1], sc2[:, 1:2],
                                       -1.0, ALU.mult, ALU.mult))
            Vb(f_l2v2)

            for ci, (nn, jj) in enumerate(((n0, j0), (n1, j1))):
                def f_l2s2(v, nn=nn, jj=jj, ci=ci):
                    rec[("l2", nn)] = ve.n + 1
                    ve.inc(v.tensor_scalar(ln2a[:, nn, :], xmid[:, nn, :], sc2[:, ci: ci + 1],
                                           sc3[:, jj: jj + 1], ALU.mult, ALU.add))
                Vb(f_l2s2)

            def f_l2mx(v, n0=n0, n1=n1):
                v.tensor_reduce(stat[:, 8 + n0 % 8: 9 + n0 % 8], ln2a[:, n0, :],
                                AX.X, ALU.max, apply_absolute_value=True)
                ve.inc(v.tensor_reduce(stat[:, 8 + n1 % 8: 9 + n1 % 8], ln2a[:, n1, :],
                                       AX.X, ALU.max, apply_absolute_value=True))
            Vb(f_l2mx)

        if _stop("s5a") or _stop("s5b"):
            return nc, tap

        def f_xsf(v):
            ve.wait(v)
            v.tensor_scalar(jrow[:], jrow[:], 0.0, None, ALU.mult)
            v.tensor_reduce(stat2[:, 5:6], stat[:, 8:16], AX.X, ALU.max)
            v.tensor_scalar(jrow[:], jrow[:], 0.0, None, ALU.mult)
            ve.inc(v.tensor_reduce(sc1[:, 0:1], stat2[:, 4:6], AX.X, ALU.max))
        Vb(f_xsf)

        def f_xst(t):
            ve.wait(t)
            mi_auto = t.transpose(psA[0:1, 0:128], sc1[:, 0:1], ident[:])
            tm.inc(mi_auto)
        Tb(f_xst)

        def f_xsr(v):
            tm.wait(v)
            v.tensor_reduce(srow[0:1, 24:25], psA[0:1, 0:128], AX.X, ALU.max)
            v.tensor_scalar(jrow[:], jrow[:], 0.0, None, ALU.mult)
            v.tensor_scalar(srow[0:1, 25:26], srow[0:1, 24:25], 1.0 / 127.0, LNEPS,
                            ALU.mult, ALU.max)
            v.tensor_scalar(jrow[:], jrow[:], 0.0, None, ALU.mult)
            for i in range(8):
                mi_auto = v.tensor_copy(srow[0:1, 40 + i: 41 + i], srow[0:1, 25:26])
            ve.inc(mi_auto)
        Vb(f_xsr)

        if _stop("s5c"):
            return nc, tap

        def f_arx(g):
            ve.wait(g)
            gsp.begin(g)
            gsp.dma(g, mxx_in.ap()[:, :], srow[0:1, 40:48])
            tw(g, gsp.tok())
            cc.inc(g.collective_compute(
                "AllReduce", ALU.max,
                replica_groups=[[0, 1, 2, 3, 4, 5, 6, 7]],
                ins=[mxx_in.ap().opt()], outs=[mxx_out.ap().opt()]))
            g.wait_ge(cc.h, cc.n)
            glp.begin(g)
            glp.dma(g, srow[0:1, 48:56], mxx_out.ap()[:, :])
            rec["xs_done"] = glp.tok()
        Gb(f_arx)

        for tname in ("xmid", "ln2", "mxx"):
            if tname in tap:
                def t_t(g, tname=tname):
                    if tname == "mxx":
                        cc.wait(g)
                        gsp.begin(g)
                        gsp.dma(g, tap["mxx"][:, :], mxx_out.ap()[:, :])
                    else:
                        src2 = xmid if tname == "xmid" else ln2a
                        se.wait(g)
                        ve.wait(g)
                        gsp.begin(g)
                        gsp.dma(g, tap[tname][:], src2[:])
                Gb(t_t)

        if _stop("s5"):
            return nc, tap

        w8p = w8Td.rearrange("(ec p) f -> p ec f", p=128)

        def f_wfl(g):
            tm.wait(g)
            se.wait(g)
            glp.begin(g)
            glp.dma(g, w8s[:, :, :], w8p)
            glp.dma(g, wsr[:], ws_row)
            glp.dma(g, fbrow[:], fb_row)
            rec["wfl"] = glp.tok()
        Gb(f_wfl)

        def f_xsc(v):
            tw(v, rec["xs_done"])
            mi_auto = v.reciprocal(srow[0:1, 56:57], srow[0:1, 48:49])
            ve.inc(mi_auto)
        Vb(f_xsc)

        def f_xbc(t):
            ve.wait(t)
            tw(t, rec["wfl"])
            t.matmul(psA[:, 0:1], onesf[0:1, :], srow[0:1, 48:49], start=True, stop=True)
            t.matmul(psA[:, 512:1024], onesf[0:1, :], wsr[0:1, 0:512], start=True, stop=True)
            mi_auto = t.matmul(psA[:, 1024:1536], onesf[0:1, :], wsr[0:1, 512:1024], start=True, stop=True)
            tm.inc(mi_auto)
        Tb(f_xbc)

        def f_xbev(v):
            tm.wait(v)
            v.tensor_copy(sc1[:, 0:1], psA[:, 0:1])
            v.tensor_copy(wsbc[:, 0:512], psA[:, 512:1024])
            v.tensor_copy(wsbc[:, 512:1024], psA[:, 1024:1536])
            v.tensor_scalar(fbp[0:1, :], fbrow[:], srow[0:1, 56:57], None, ALU.mult)
            ve.inc(v.reciprocal(sc2[:, 0:1], sc1[:, 0:1]))
        Vb(f_xbev)

        for ech in range(8):
            def f_wfb(v, ech=ech):
                if ech == 0:
                    tm.wait(v)
                    se.wait(v)
                ve.wait(v)
                v.tensor_copy(qtrot[:, 1, :], w8s[:, ech, :])
                ve.inc(v.tensor_tensor(wfbf[:, ech, :], qtrot[:, 1, :], wsbc[:], ALU.mult))
            Vb(f_wfb)

        def f_idl(g):
            ve.wait(g)
            glp.begin(g)
            glp.dma(g, identb2[:], identb_c.ap())
            rec["idl"] = glp.tok()
        Gb(f_idl)

        for n1 in range(NCH):
            def f_qza(v, n1=n1):
                ve.wait(v)
                if n1 == 0:
                    slp.wait_all(v)
                if n1 >= 2:
                    ve.wait(v, rec[("qtr", n1 - 2)])
                mi_auto = v.tensor_scalar(psA[:, 3072:4096], ln2a[:, n1, :],
                                          sc2[:, 0:1], None, ALU.mult)
                ve.inc(mi_auto)
            Vb(f_qza)

            def f_qzs(s, n1=n1):
                ve.wait(s)
                se.inc(s.activation(qsg[:], psA[:, 3072:4096], AF.Sign))
            Sb(f_qzs)

            def f_qz(v, n1=n1):
                se.wait(v)
                v.tensor_tensor(psA[:, 3072:4096], psA[:, 3072:4096], qsg[:], ALU.mult)
                v.tensor_scalar(qw[:], psA[:, 3072:4096], 128.0, None, ALU.add)
                v.scalar_tensor_tensor(i8rot[:, n1 % 2, :], qw[:], -128.0, qsg[:],
                                       ALU.add, ALU.mult)
                rec[("qz", n1)] = ve.n + 1
                ve.inc(v.tensor_copy(qtrot[:, n1 % 2, :], i8rot[:, n1 % 2, :]))
            Vb(f_qz)

            def f_qtr_t(t, n1=n1):
                ve.wait(t, rec[("qz", n1)])
                if n1 == 0:
                    tm.wait(t)
                if n1 >= 2:
                    ve.wait(t, rec[("qtr", n1 - 2)])
                base = 4096 + 1024 * (n1 % 2)
                if n1 == 0:
                    tw(t, rec["idl"])
                for ech in range(8):
                    mi_auto = t.transpose(
                        psAb[:, base + 128 * ech: base + 128 * (ech + 1)],
                        qtrot[:, n1 % 2, 128 * ech: 128 * (ech + 1)], identb2[:])
                tm.inc(mi_auto)
            Tb(f_qtr_t)

            def f_qtr_e(v, n1=n1):
                tm.wait(v)
                base = 4096 + 1024 * (n1 % 2)
                for ech in range(8):
                    mi_auto = v.tensor_copy(
                        attn_fm[:, ech, 128 * n1: 128 * (n1 + 1)],
                        psAb[:, base + 128 * ech: base + 128 * (ech + 1)])
                rec[("qtr", n1)] = ve.n + 1
                ve.inc(mi_auto)
            Vb(f_qtr_e)

            def f_fmm(t, n1=n1):
                ve.wait(t, rec[("qtr", n1)])
                if n1 == 0:
                    ve.wait(t)
                if n1 >= 2:
                    ve.wait(t, rec[("fev", n1 - 2)])
                pb = 1024 * (n1 % 2)
                for ech in range(8):
                    for fh2 in range(2):
                        t.matmul(psA[:, pb + 512 * fh2: pb + 512 * (fh2 + 1)],
                                 attn_fm[:, ech, 128 * n1: 128 * (n1 + 1)],
                                 wfbf[:, ech, 512 * fh2: 512 * (fh2 + 1)],
                                 start=(ech == 0), stop=False, skip_group_check=True)
                for fh2 in range(2):
                    mi_auto = t.matmul(psA[:, pb + 512 * fh2: pb + 512 * (fh2 + 1)],
                             onesr[0:1, 0:128], fbp[0:1, 512 * fh2: 512 * (fh2 + 1)],
                             start=False, stop=True, skip_group_check=True)
                rec[("fmm", n1)] = tm.n + 1
                tm.inc(mi_auto)
            Tb(f_fmm)

            def f_fev(v, n1=n1):
                tm.wait(v, rec[("fmm", n1)])
                pb = 1024 * (n1 % 2)
                v.scalar_tensor_tensor(outa[:, n1, 0:512], psA[:, pb: pb + 512],
                                       sc1[:, 0:1], xmid[:, n1, 0:512], ALU.mult, ALU.add)
                rec[("fev", n1)] = ve.n + 1
                ve.inc(v.scalar_tensor_tensor(outa[:, n1, 512:1024],
                                              psA[:, pb + 512: pb + 1024],
                                              sc1[:, 0:1], xmid[:, n1, 512:1024],
                                              ALU.mult, ALU.add))
            Vb(f_fev)

            def f_yst(g, n1=n1):
                ve.wait(g, rec[("fev", n1)])
                gsp.begin(g)
                gsp.dma(g, y_out.rearrange("(c p) f -> c p f", p=128)[n1],
                        outa[:, n1, :])
            Gb(f_yst)

        def f_fin(g):
            gsp.wait_all(g)
        Gb(f_fin)

    return nc, tap



def _shard_inputs(inputs):
    x = np.ascontiguousarray(np.asarray(inputs["L0_attn_x"], np.float32))
    Wq = np.asarray(inputs["L0_attn_Wq"], np.float32)
    Wk = np.asarray(inputs["L0_attn_Wk"], np.float32)
    Wv = np.asarray(inputs["L0_attn_Wv"], np.float32)
    Wo = np.asarray(inputs["L0_attn_Wo"], np.float32)
    bqv = np.asarray(inputs["L0_attn_bq"], np.float32)
    bkv = np.asarray(inputs["L0_attn_bk"], np.float32)
    bvv = np.asarray(inputs["L0_attn_bv"], np.float32)
    bov = np.asarray(inputs["L0_attn_bo"], np.float32)
    proj = np.asarray(inputs["L0_attn_proj"], np.float32)
    w8 = np.asarray(inputs["L0_ffn_weight_int8"], np.int8)
    ws = np.asarray(inputs["L0_ffn_weight_scale"], np.float32)
    fb = np.asarray(inputs["L0_ffn_bias"], np.float32)

    wqTh = np.ascontiguousarray(Wq.T)
    woTh = np.ascontiguousarray(Wo.T)
    pT = np.ascontiguousarray(proj.T)
    projTh = np.zeros((128, 512), np.float32)
    projTh[0:64, 0:256] = pT
    projTh[64:128, 256:512] = pT
    w8Th = np.ascontiguousarray(w8.T)
    bo_rh = np.ascontiguousarray(bov.reshape(1, E))
    ws_rh = np.ascontiguousarray(ws.reshape(1, E))
    fb_rh = np.ascontiguousarray(fb.reshape(1, E))

    maps = []
    for c in range(8):
        b, hf = c // 2, c % 2
        hsl = slice(512 * hf, 512 * (hf + 1))
        maps.append({
            "x_own": np.ascontiguousarray(x[b, 2048 * hf: 2048 * (hf + 1)]),
            "x_par": np.ascontiguousarray(x[b, 2048 * (1 - hf): 2048 * (2 - hf)]),
            "wqT": wqTh,
            "wkT": np.ascontiguousarray(Wk[hsl].T),
            "wvT": np.ascontiguousarray(Wv[hsl].T),
            "woT": woTh,
            "projT": projTh,
            "bq_row": np.ascontiguousarray(bqv.reshape(1, E)),
            "bk_row": np.ascontiguousarray(bkv[hsl].reshape(1, 512)),
            "bv_row": np.ascontiguousarray(bvv[hsl].reshape(1, 512)),
            "bo_row": bo_rh,
            "w8T": w8Th,
            "ws_row": ws_rh,
            "fb_row": fb_rh,
        })
    return maps


def kernel(**inputs):
    maps = _shard_inputs(inputs)
    nc, _ = build_nc(taps=_TAPS)
    res = run_bass_kernel_spmd(nc, maps, core_ids=list(range(8)))
    out = np.empty((B, NSEQ, E), np.float32)
    for c in range(8):
        b, hf = c // 2, c % 2
        out[b, 2048 * hf: 2048 * (hf + 1)] = res.results[c]["y"]
    if _TAPS:
        kernel.last = res
    return out
